# version 05
# baseline (speedup 1.0000x reference)
import numpy as np
import ml_dtypes
from contextlib import ExitStack

import concourse.bass as bass
import concourse.mybir as mybir
from concourse.bass_utils import run_bass_kernel_spmd

F32 = mybir.dt.float32
BF16 = mybir.dt.bfloat16
AF = mybir.ActivationFunctionType
ALU = mybir.AluOpType
AX = mybir.AxisListType

B, T, C = 2, 8192, 1024
NH, HS, KVL = 16, 64, 128
D, H, E = 1024, 2048, 8
NCORE = 8
NST = 8
STQ = 256
TOK = NST * STQ
TAU = 512
NTAU = TOK // TAU
DCH = D // 128
HCH = H // 128
NEXP = 9
EPS = 1e-5

bfnp = ml_dtypes.bfloat16
ENGINES = ("pe", "ve", "ac", "gp", "sy")


class Buf:
    __slots__ = ("w", "r", "snap")

    def __init__(self):
        self.w = []
        self.r = []
        self.snap = None

    def wdeps(self):
        return self.r + self.w

    def rdeps(self):
        return list(self.w)

    def note_w(self, tok):
        if tok is not None:
            self.w = [tok]
            self.r = []
            self.snap = None

    def begin_group(self):
        self.snap = self.wdeps()
        self.w = []
        self.r = []

    def note_w_group(self, tok):
        if tok is not None:
            self.w.append(tok)

    def note_r(self, tok):
        if tok is not None:
            self.r.append(tok)


class Prog:
    def __init__(self, semnames):
        self.q = {e: [] for e in ENGINES}
        self.ctr = {s: 0 for s in semnames}
        self.waited = {e: {s: 0 for s in semnames} for e in ENGINES}

    BURST = ("lm0", "lmo", "lms", "lmk", "lmm", "lmx0", "lmx1", "lmx2")

    def op(self, eng, build, deps=(), inc=None, amt=1):
        waits = []
        for d in deps:
            if d is None:
                continue
            s, v, peng = d
            if eng == "pe" and peng == "pe":
                continue
            if s in self.BURST:
                v = self.ctr[s]
            if v > self.waited[eng][s]:
                self.waited[eng][s] = v
                waits.append((s, v))
        tok = None
        if inc is not None:
            self.ctr[inc] += amt
            tok = (inc, self.ctr[inc], eng)
        self.q[eng].append((build, waits, inc, amt))
        return tok


def _local_positions():
    pos = {}
    for ct in range(4):
        j = np.arange(NST)[:, None]
        q = np.arange(STQ)[None, :]
        pos[ct] = (1024 * j + 256 * ct + q).reshape(-1)
    return pos


def make_masks(ct):
    s = np.arange(128)
    q = np.arange(STQ)
    out = np.zeros((128, 16 * STQ), dtype=np.float32)
    for c in range(8):
        keyr = 128 * c + s[:, None]
        qg = 256 * ct + q[None, :]
        m = np.where(keyr <= qg, 0.0, -1e30)
        out[:, 2 * c * STQ:(2 * c + 1) * STQ] = m
        out[:, (2 * c + 1) * STQ:(2 * c + 2) * STQ] = m
    return np.ascontiguousarray(out.astype(bfnp))


def gather_chunk_src(k):
    ct = (k % 8) // 2
    j = k // 8
    return ct * TOK + j * STQ + (k % 2) * 128


def build_program(phases=("ffn1", "p2", "attn", "ffn2"), debug=False, experts=tuple(range(NEXP)), attn_level=5):
    nc = bass.Bass("TRN2", target_bir_lowering=False)
    dr = {}

    def din(name, shape, dt=F32):
        dr[name] = nc.dram_tensor(name, list(shape), dt, kind="ExternalInput").ap()
        return dr[name]

    din("xT", (D, TOK))
    din("wcat", (NEXP, HCH, 128, 2048))
    din("w2cat", (NEXP, HCH, 128, D))
    din("wdqT", (DCH, 128, 128))
    din("wuqT", (128, D))
    din("wdq", (128, D))
    din("wdkvT", (DCH, 128, 128))
    din("wuk", (DCH, 128, KVL))
    din("wuv", (DCH, 128, KVL))
    din("woT", (DCH, 128, D))
    din("routT", (DCH, 128, E))
    din("bias8", (E, 1))
    din("masks", (128, 16 * STQ), BF16)
    din("identb", (128, 128), BF16)
    din("identf", (128, 128))
    din("onesr", (1, 128))
    din("onescb", (128, 1), BF16)
    din("onescf", (128, 1))
    din("onehot", (E, E * 128), BF16)
    out_d = nc.dram_tensor("outT", [D, TOK], F32, kind="ExternalOutput").ap()
    cc_in = nc.dram_tensor("cc_in", [TOK, KVL], BF16, kind="Internal")
    cc_out = nc.dram_tensor("cc_out", [4 * TOK, KVL], BF16, kind="Internal")

    es = ExitStack()
    S = lambda n, sh, dt: es.enter_context(nc.sbuf_tensor(n, list(sh), dt))

    resid = S("resid", (128, DCH * TOK), F32)
    xln = S("xln", (128, DCH * TOK), BF16)
    eo = S("eo", (128, HCH * 1024), BF16)
    ckvT = S("ckvT", (128, T), BF16)
    ckvM = S("ckvM", (128, T), BF16)
    wcat_s = S("wcat_s", (128, 2 * 2048), BF16)
    w2b_s = S("w2b_s", (128, 4 * 512), BF16)
    cwb_s = S("cwb_s", (128, 2 * TAU), BF16)
    keff_s = S("keff_s", (128, DCH * 128), BF16)
    veff_s = S("veff_s", (128, D), BF16)
    routT_s = S("routT_s", (128, DCH * E), BF16)
    bias8_s = S("bias8_s", (E, 1), F32)
    identb_s = S("identb_s", (128, 128), BF16)
    identf_s = S("identf_s", (128, 128), F32)
    onesr_s = S("onesr_s", (1, 128), F32)
    onescb_s = S("onescb_s", (128, 1), BF16)
    onescf_s = S("onescf_s", (128, 1), F32)
    t2_s = S("t2_s", (128, 48), F32)
    sq_s = S("sq_s", (128, TAU), F32)
    sgub = S("sgub", (128, 4 * TAU), BF16)
    stat = S("stat", (1, 3 * TAU), F32)
    zr_s = S("zr_s", (1, 1024), F32)
    rows = S("rows", (1, 24), F32)
    sc_s = S("sc_s", (1, 16), F32)
    qTst_s = S("qTst_s", (128, DCH * STQ), BF16)
    qlT_s = S("qlT_s", (128, 4 * STQ), BF16)
    qa_s = S("qa_s", (128, STQ), BF16)
    rzb_s = S("rzb_s", (128, 2 * STQ), F32)
    negb_s = S("negb_s", (128, 2), F32)
    ctxn_s = S("ctxn_s", (128, 2 * STQ), BF16)
    stg_s = S("stg_s", (128, 2 * 128), BF16)
    wdqT_s = sgub[:, 0:1024]
    wuqT_s = sgub[:, 1024:2048]
    wdkvT_s = sgub[:, 0:1024]
    lnst_s = xln[:, 0:DCH * STQ]
    cw8 = qTst_s[0:E, 0:TOK]
    l8_s = sq_s[0:E, 0:TAU]
    masks_s = eo[:, 0:16 * STQ]
    probs_s = eo[:, 16 * STQ:24 * STQ]
    onehot_s = ckvM[0:E, 0:E * 128]

    PP = [es.enter_context(nc.psum_tensor(f"pp{i}", [128, 1024], F32))
          for i in range(4)]

    def bk(i, p0=0, p1=128, f0=0, f1=512):
        return PP[i // 2][p0:p1, (i % 2) * 512 + f0:(i % 2) * 512 + f1]

    sems = ("lm0", "lmo", "lms", "lmk", "lmm", "lmx0", "lmx1", "lmx2",
            "wca", "wcb", "w2a", "w2b", "w2c", "w2d", "pe", "ve", "ac", "cc", "st")
    P = Prog(sems)
    bufs = {}

    def buf(name):
        if name not in bufs:
            bufs[name] = Buf()
        return bufs[name]

    BK = [buf(f"bank_{i}") for i in range(8)]
    RES = [buf(f"resid_{k}") for k in range(DCH)]

    def dma(queue, dst_ap, src_ap, obufs=(), ibufs=(), sem=None, group=False):
        assert sem is not None
        deps = []
        for ob in obufs:
            deps += (ob.snap if (group and ob.snap is not None) else ob.wdeps())
        for ib in ibufs:
            deps += ib.rdeps()
        tok = P.op(queue, lambda g, d=dst_ap, s=src_ap: g.dma_start(out=d, in_=s),
                   deps=deps, inc=sem, amt=16)
        for ob in obufs:
            (ob.note_w_group if group else ob.note_w)(tok)
        for ib in ibufs:
            ib.note_r(tok)
        return tok

    def mm(out_ap, lhsT_ap, rhs_ap, obank, ibufs=(), start=True, stop=True, inc=True,
           tile_position=None):
        deps = []
        if start:
            deps += obank.wdeps()
        for ib in ibufs:
            deps += ib.rdeps()

        def b(t, o=out_ap, l=lhsT_ap, r=rhs_ap, st=start, sp=stop, tp=tile_position):
            if tp is None:
                return t.matmul(o, l, r, start=st, stop=sp)
            return t.matmul(o, l, r, start=st, stop=sp, tile_position=tp)

        tok = P.op("pe", b, deps=deps, inc=("pe" if inc else None))
        if stop and inc:
            obank.note_w(tok)
        for ib in ibufs:
            ib.note_r(tok)
        return tok

    def vop(build, obufs=(), ibufs=(), eng="ve", group=False):
        sem = "ve" if eng == "ve" else "ac"
        deps = []
        for ob in obufs:
            deps += (ob.snap if (group and ob.snap is not None) else ob.wdeps())
        for ib in ibufs:
            deps += ib.rdeps()
        tok = P.op(eng, build, deps=deps, inc=sem)
        for ob in obufs:
            (ob.note_w_group if group else ob.note_w)(tok)
        for ib in ibufs:
            ib.note_r(tok)
        return tok

    cb = {}
    for name, tile in [("identb", identb_s), ("identf", identf_s),
                       ("onesr", onesr_s), ("onescb", onescb_s), ("onescf", onescf_s),
                       ("bias8", bias8_s)]:
        cb[name] = buf("c_" + name)
        dma("sy", tile[:, :], dr[name], obufs=[cb[name]], sem="lm0")
    ROUTT = buf("routT")
    ROUTT.begin_group()
    for k in range(DCH):
        dma("gp", routT_s[:, k * E:(k + 1) * E], dr["routT"][k], obufs=[ROUTT],
            group=True, sem="lmx0")
    for k in range(DCH):
        dma("sy", resid[:, k * TOK:(k + 1) * TOK], dr["xT"][k * 128:(k + 1) * 128, :],
            obufs=[RES[k]], sem="lm0")

    CKVT = buf("ckvT")
    CKVM = buf("ckvM")
    LNOUT = [buf(f"lnout{k}") for k in range(DCH)]
    SG = [buf(f"sg{i}") for i in range(4)]
    EO = {(m, ti): buf(f"eo{m}_{ti}") for m in range(HCH) for ti in range(2)}

    SG[0].begin_group()
    SG[1].begin_group()
    SG[2].begin_group()
    SG[3].begin_group()
    for k in range(DCH):
        dma("gp", sgub[:, k * 128:(k + 1) * 128], dr["wuk"][k], obufs=[SG[0], SG[1]],
            group=True, sem="lmx0")
        dma("gp", wuqT_s[:, k * 128:(k + 1) * 128],
            dr["wuqT"][:, k * 128:(k + 1) * 128], obufs=[SG[2], SG[3]], group=True,
            sem="lmx0")
    dma("gp", ckvT[:, 0:D], dr["wdq"], obufs=[CKVT], sem="lmx0")
    LNOUT[0].begin_group()
    for k in range(DCH):
        dma("gp", xln[:, k * 128:(k + 1) * 128], dr["wuv"][k], obufs=[LNOUT[0]],
            group=True, sem="lmx0")

    for c in range(DCH):
        for h2 in range(2):
            dma("gp", eo[:, 8192 + c * 1024 + h2 * 512: 8192 + c * 1024 + (h2 + 1) * 512],
                dr["woT"][c][:, h2 * 512:(h2 + 1) * 512], obufs=[EO[(8 + c, h2)]],
                group=False, sem="lmx0")

    for c in range(DCH):
        for half in range(2):
            i = half
            mm(bk(i), wuqT_s[:, c * 128:(c + 1) * 128],
               ckvT[:, half * 512:(half + 1) * 512], BK[i],
               ibufs=[SG[2], SG[3], CKVT])
            vop(lambda v, c=c, h=half, i=i: v.tensor_copy(
                eo[:, c * 1024 + h * 512:c * 1024 + (h + 1) * 512], bk(i)),
                obufs=[EO[(c, half)]], ibufs=[BK[i]])
    KEFF = buf("keff")
    KEFF.begin_group()
    for r in range(DCH):
        for c in range(DCH):
            mm(bk(2, f1=128), eo[:, c * 1024 + r * 128:c * 1024 + (r + 1) * 128],
               sgub[:, c * 128:(c + 1) * 128], BK[2],
               ibufs=[EO[(c, r // 4)], SG[0], SG[1]], start=(c == 0),
               stop=(c == DCH - 1), inc=(c == DCH - 1))
        vop(lambda v, r=r: v.tensor_scalar(
            out=keff_s[:, r * 128:(r + 1) * 128], in0=bk(2, f1=128),
            scalar1=0.125, scalar2=None, op0=ALU.mult), obufs=[KEFF], ibufs=[BK[2]],
            group=True)
    VEFF = buf("veff")
    VEFF.begin_group()
    for half in range(2):
        for c in range(DCH):
            wo_t = eo[:, 8192 + c * 1024 + half * 512: 8192 + c * 1024 + (half + 1) * 512]
            mm(bk(3), xln[:, c * 128:(c + 1) * 128], wo_t, BK[3],
               ibufs=[LNOUT[0], EO[(8 + c, half)]], start=(c == 0), stop=(c == DCH - 1),
               inc=(c == DCH - 1))
        vop(lambda v, h=half: v.tensor_copy(
            veff_s[:, h * 512:(h + 1) * 512], bk(3)), obufs=[VEFF], ibufs=[BK[3]],
            group=True)

    STATB = {n: buf("st_" + n) for n in
             ("mean", "var", "m2", "lnv", "rstd", "mr")}
    SQS = [buf("sqslot0")]
    def emit_ln(tok_off, ntok, dst, dst_chunk_stride, dst_off, ba, bb1_i, bb2_i, outb):
        for k in range(DCH):
            rs = resid[:, k * TOK + tok_off: k * TOK + tok_off + ntok]
            vop(lambda v, rs=rs, n=ntok: v.tensor_tensor(
                out=sq_s[:, 0:n], in0=rs, in1=rs, op=ALU.mult),
                obufs=[SQS[0]], ibufs=[RES[k]])
            mm(bk(ba, 0, 1, 0, ntok), onescf_s[:, :], rs, BK[ba],
               ibufs=[cb["onescf"], RES[k]], start=(k == 0), stop=(k == DCH - 1),
               inc=(k == DCH - 1))
            mm(bk(ba, 32, 33, 0, ntok), onescf_s[:, :], sq_s[:, 0:ntok],
               BK[ba], ibufs=[cb["onescf"], SQS[0]], start=(k == 0),
               stop=(k == DCH - 1), inc=True)
        vop(lambda v, n=ntok: v.tensor_scalar(
            out=stat[0:1, 0:n], in0=bk(ba, 0, 1, 0, n), scalar1=1.0 / D, scalar2=None,
            op0=ALU.mult), obufs=[STATB["mean"]], ibufs=[BK[ba]])
        vop(lambda v, n=ntok: v.tensor_scalar(
            out=stat[0:1, TAU:TAU + n], in0=bk(ba, 32, 33, 0, n), scalar1=1.0 / D,
            scalar2=EPS, op0=ALU.mult, op1=ALU.add), obufs=[STATB["var"]],
            ibufs=[BK[ba]])
        vop(lambda v, n=ntok: v.tensor_tensor(
            out=stat[0:1, 2 * TAU:2 * TAU + n], in0=stat[0:1, 0:n],
            in1=stat[0:1, 0:n], op=ALU.mult), obufs=[STATB["m2"]],
            ibufs=[STATB["mean"]])
        vop(lambda v, n=ntok: v.tensor_tensor(
            out=stat[0:1, TAU:TAU + n], in0=stat[0:1, TAU:TAU + n],
            in1=stat[0:1, 2 * TAU:2 * TAU + n], op=ALU.subtract),
            obufs=[STATB["var"]], ibufs=[STATB["m2"], STATB["var"]])
        vop(lambda s, n=ntok: s.activation(
            out=stat[0:1, 2 * TAU:2 * TAU + n], in_=stat[0:1, TAU:TAU + n],
            func=AF.Ln),
            obufs=[STATB["lnv"]], ibufs=[STATB["var"]], eng="ac")
        vop(lambda s, n=ntok: s.activation(
            out=stat[0:1, TAU:TAU + n], in_=stat[0:1, 2 * TAU:2 * TAU + n],
            func=AF.Exp, scale=-0.5), obufs=[STATB["rstd"]], ibufs=[STATB["lnv"]], eng="ac")
        vop(lambda v, n=ntok: v.tensor_tensor(
            out=stat[0:1, 2 * TAU:2 * TAU + n], in0=stat[0:1, 0:n],
            in1=stat[0:1, TAU:TAU + n], op=ALU.mult), obufs=[STATB["mr"]],
            ibufs=[STATB["mean"], STATB["rstd"]])
        vop(lambda v, n=ntok: v.tensor_scalar(
            out=stat[0:1, 2 * TAU:2 * TAU + n],
            in0=stat[0:1, 2 * TAU:2 * TAU + n],
            scalar1=-1.0, scalar2=None, op0=ALU.mult), obufs=[STATB["mr"]],
            ibufs=[STATB["mr"]])
        off2 = ntok if bb1_i == bb2_i else 0
        mm(bk(bb1_i, 0, 128, 0, ntok), onesr_s[:, :], stat[0:1, TAU:TAU + ntok],
           BK[bb1_i], ibufs=[cb["onesr"], STATB["rstd"]])
        mm(bk(bb2_i, 0, 128, off2, off2 + ntok), onesr_s[:, :],
           stat[0:1, 2 * TAU:2 * TAU + ntok], BK[bb2_i],
           ibufs=[cb["onesr"], STATB["mr"]])
        for k in range(DCH):
            rs = resid[:, k * TOK + tok_off: k * TOK + tok_off + ntok]
            vop(lambda v, rs=rs, n=ntok, b1=bb1_i: v.tensor_tensor(
                out=sq_s[:, 0:n], in0=rs, in1=bk(b1, 0, 128, 0, n),
                op=ALU.mult), obufs=[SQS[0]], ibufs=[BK[bb1_i], RES[k]])
            vop(lambda v, k=k, n=ntok, dst=dst, st=dst_chunk_stride, do=dst_off,
                b2=bb2_i, o2=off2: v.tensor_tensor(
                    out=dst[:, k * st + do: k * st + do + n],
                    in0=sq_s[:, 0:n], in1=bk(b2, 0, 128, o2, o2 + n),
                    op=ALU.add), obufs=[outb[k]], ibufs=[BK[bb2_i], SQS[0]])

    dbg_state = {}
    CW8 = buf("cwqt")
    T2 = buf("t2")
    L8 = SQS[0]
    CWB = [buf("cwb0"), buf("cwb1")]
    WCAT = [buf("wcat0"), buf("wcat1")]
    W2B = [buf(f"w2b{i}") for i in range(4)]

    def emit_ffn():
        CW8.begin_group()
        dma("sy", onehot_s, dr["onehot"], obufs=[CKVM], sem="lmo")
        for tau in range(NTAU):
            to = tau * TAU
            emit_ln(to, TAU, xln, TOK, to, 4, 5, 6, LNOUT)
            for k in range(DCH):
                mm(bk(7, 0, E, 0, TAU), routT_s[:, k * E:(k + 1) * E],
                   xln[:, k * TOK + to: k * TOK + to + TAU], BK[7],
                   ibufs=[ROUTT, LNOUT[k]], start=(k == 0), stop=(k == DCH - 1),
                   inc=(k == DCH - 1))
            vop(lambda v: v.tensor_scalar(
                out=l8_s[0:E, 0:TAU], in0=bk(7, 0, E, 0, TAU),
                scalar1=bias8_s[0:E, 0:1], scalar2=None, op0=ALU.add),
                obufs=[L8], ibufs=[BK[7], cb["bias8"]])
            for cc_ in range(TAU // 128):
                mm(bk(6, 0, 128, 0, E), l8_s[0:E, cc_ * 128:(cc_ + 1) * 128],
                   identf_s[0:E, 0:E], BK[6], ibufs=[L8, cb["identf"]])
                lt = t2_s[:, 0:8]
                m1 = t2_s[:, 8:9]
                m2 = t2_s[:, 9:10]
                dd = t2_s[:, 10:11]
                aa = t2_s[:, 11:12]
                bb_ = t2_s[:, 12:13]
                k1 = t2_s[:, 16:24]
                k2 = t2_s[:, 24:32]
                l2 = t2_s[:, 32:40]
                vop(lambda v: v.tensor_copy(lt, bk(6, 0, 128, 0, E)), obufs=[T2],
                    ibufs=[BK[6]])
                vop(lambda v: v.tensor_reduce(out=m1, in_=lt, axis=AX.X, op=ALU.max),
                    obufs=[T2], ibufs=[T2])
                vop(lambda v: v.tensor_scalar(out=k1, in0=lt, scalar1=m1, scalar2=None,
                                              op0=ALU.is_ge), obufs=[T2], ibufs=[T2])
                vop(lambda v: v.tensor_scalar(out=l2, in0=k1, scalar1=-1e30,
                                              scalar2=None, op0=ALU.mult),
                    obufs=[T2], ibufs=[T2])
                vop(lambda v: v.tensor_tensor(out=l2, in0=l2, in1=lt, op=ALU.add),
                    obufs=[T2], ibufs=[T2])
                vop(lambda v: v.tensor_reduce(out=m2, in_=l2, axis=AX.X, op=ALU.max),
                    obufs=[T2], ibufs=[T2])
                vop(lambda v: v.tensor_scalar(out=k2, in0=l2, scalar1=m2, scalar2=None,
                                              op0=ALU.is_ge), obufs=[T2], ibufs=[T2])
                vop(lambda v: v.tensor_tensor(out=dd, in0=m1, in1=m2, op=ALU.subtract),
                    obufs=[T2], ibufs=[T2])
                vop(lambda s: s.activation(out=aa, in_=dd, func=AF.Sigmoid),
                    obufs=[T2], ibufs=[T2], eng="ac")
                vop(lambda s: s.activation(out=bb_, in_=dd, func=AF.Sigmoid, scale=-1.0),
                    obufs=[T2], ibufs=[T2], eng="ac")
                vop(lambda v: v.tensor_scalar(out=k1, in0=k1, scalar1=aa, scalar2=None,
                                              op0=ALU.mult), obufs=[T2], ibufs=[T2])
                vop(lambda v: v.tensor_scalar(out=k2, in0=k2, scalar1=bb_, scalar2=None,
                                              op0=ALU.mult), obufs=[T2], ibufs=[T2])
                vop(lambda v: v.tensor_tensor(out=k1, in0=k1, in1=k2, op=ALU.add),
                    obufs=[T2], ibufs=[T2])
                vop(lambda v: v.tensor_scalar(out=k1, in0=k1, scalar1=0.5, scalar2=None,
                                              op0=ALU.mult), obufs=[T2], ibufs=[T2])
                mm(bk(6, 0, E, 128, 256), k1, identf_s[:, :], BK[6],
                   ibufs=[T2, cb["identf"]])
                vop(lambda v, to=to, cc_=cc_: v.tensor_copy(
                    cw8[0:E, to + cc_ * 128: to + (cc_ + 1) * 128],
                    bk(6, 0, E, 128, 256)), obufs=[CW8], ibufs=[BK[6]], group=True)

        for e in experts:
            for tp in range(2):
                if e < E:
                    for ti in range(2):
                        tau = 2 * tp + ti
                        mm(bk(5, 0, 128, 0, TAU), onehot_s[0:E, e * 128:(e + 1) * 128],
                           cw8[0:E, tau * TAU:(tau + 1) * TAU], BK[5],
                           ibufs=[CKVM, CW8])
                        vop(lambda v, ti=ti: v.tensor_copy(
                            cwb_s[:, ti * TAU:(ti + 1) * TAU], bk(5, 0, 128, 0, TAU)),
                            obufs=[CWB[ti]], ibufs=[BK[5]])
                for m in range(HCH):
                    ws = m % 2
                    dma("gp", wcat_s[:, ws * 2048:(ws + 1) * 2048], dr["wcat"][e][m],
                        obufs=[WCAT[ws]], sem=("wca", "wcb")[ws])
                    if dbg_state.get("w") is not None and m == 0:
                        dma("sy", dbg_state["w"], wcat_s[:, 0:2048], ibufs=[WCAT[0]],
                            sem="st")
                        dbg_state["w"] = None
                    for k in range(DCH):
                        w1k = wcat_s[:, ws * 2048 + k * 128: ws * 2048 + (k + 1) * 128]
                        w3k = wcat_s[:, ws * 2048 + 1024 + k * 128:
                                     ws * 2048 + 1024 + (k + 1) * 128]
                        for ti in range(2):
                            tau = 2 * tp + ti
                            xk = xln[:, k * TOK + tau * TAU: k * TOK + (tau + 1) * TAU]
                            mm(bk(ti), w1k, xk, BK[ti],
                               ibufs=[WCAT[ws], LNOUT[k]], start=(k == 0),
                               stop=(k == DCH - 1), inc=(k == DCH - 1))
                            mm(bk(2 + ti), w3k, xk, BK[2 + ti],
                               ibufs=[WCAT[ws], LNOUT[k]], start=(k == 0),
                               stop=(k == DCH - 1), inc=(k == DCH - 1))
                    for ti in range(2):
                        sl = (2 * m + ti) % 4
                        vop(lambda s, ti=ti, sl=sl: s.activation(
                            out=sgub[:, sl * TAU:(sl + 1) * TAU], in_=bk(ti),
                            func=AF.Silu), obufs=[SG[sl]], ibufs=[BK[ti]], eng="ac")
                        if dbg_state.get("sg") is not None and m == 0 and ti == 1:
                            dma("sy", dbg_state["sg"], sgub[:, 0:1024],
                                ibufs=[SG[0], SG[1]], sem="st")
                            dbg_state["sg"] = None
                        eob = EO[(m, ti)]
                        vop(lambda v, m=m, ti=ti, sl=sl: v.tensor_tensor(
                            out=eo[:, m * 1024 + ti * TAU: m * 1024 + (ti + 1) * TAU],
                            in0=sgub[:, sl * TAU:(sl + 1) * TAU],
                            in1=bk(2 + ti), op=ALU.mult), obufs=[eob],
                            ibufs=[SG[sl], BK[2 + ti]])
                        if e < E:
                            vop(lambda v, m=m, ti=ti: v.tensor_tensor(
                                out=eo[:, m * 1024 + ti * TAU: m * 1024 + (ti + 1) * TAU],
                                in0=eo[:, m * 1024 + ti * TAU: m * 1024 + (ti + 1) * TAU],
                                in1=cwb_s[:, ti * TAU:(ti + 1) * TAU], op=ALU.mult),
                                obufs=[eob], ibufs=[eob, CWB[ti]])
                        else:
                            vop(lambda v, m=m, ti=ti: v.tensor_scalar(
                                out=eo[:, m * 1024 + ti * TAU: m * 1024 + (ti + 1) * TAU],
                                in0=eo[:, m * 1024 + ti * TAU: m * 1024 + (ti + 1) * TAU],
                                scalar1=0.5, scalar2=None, op0=ALU.mult),
                                obufs=[eob], ibufs=[eob])
                if dbg_state.get("eo") is not None and tp == 0 and e == experts[0]:
                    dma("sy", dbg_state["eo"], eo[:, :],
                        ibufs=[EO[(m_, t_)] for m_ in range(HCH) for t_ in range(2)],
                        sem="st")
                    dbg_state["eo"] = None
                for dh in range(2):
                    for m in range(HCH):
                        ws = m % 4
                        dma("gp", w2b_s[:, ws * 512:(ws + 1) * 512],
                            dr["w2cat"][e][m][:, dh * 512:(dh + 1) * 512],
                            obufs=[W2B[ws]], sem=("w2a", "w2b", "w2c", "w2d")[ws])
                        for d4 in range(4):
                            for ti in range(2):
                                zi = d4 * 2 + ti
                                mm(bk(zi),
                                   w2b_s[:, ws * 512 + d4 * 128: ws * 512 + (d4 + 1) * 128],
                                   eo[:, m * 1024 + ti * TAU: m * 1024 + (ti + 1) * TAU],
                                   BK[zi], ibufs=[W2B[ws], EO[(m, ti)]],
                                   start=(m == 0), stop=(m == HCH - 1),
                                   inc=(m == HCH - 1) or (d4 == 3 and ti == 1))
                    for d4 in range(4):
                        for ti in range(2):
                            zi = d4 * 2 + ti
                            dchunk = dh * 4 + d4
                            tau = 2 * tp + ti
                            vop(lambda v, zi=zi, dc=dchunk, tau=tau: v.tensor_tensor(
                                out=resid[:, dc * TOK + tau * TAU:
                                          dc * TOK + (tau + 1) * TAU],
                                in0=resid[:, dc * TOK + tau * TAU:
                                          dc * TOK + (tau + 1) * TAU],
                                in1=bk(zi), op=ALU.add),
                                obufs=[RES[dchunk]], ibufs=[BK[zi], RES[dchunk]])

    def emit_p2():
        CCIN = buf("ccin")
        CCIN.begin_group()
        STGB = [buf("stg0"), buf("stg1")]
        SG[0].begin_group()
        SG[1].begin_group()
        for k in range(DCH):
            dma("gp", wdkvT_s[:, k * 128:(k + 1) * 128], dr["wdkvT"][k],
                obufs=[SG[0], SG[1]], group=True, sem="lmx1")
        for tau in range(NTAU):
            to = tau * TAU
            emit_ln(to, TAU, xln, TOK, to, 4, 5, 6, LNOUT)
            for tt in range(TAU // 128):
                gto = to + tt * 128
                sl = tt % 2
                for k in range(DCH):
                    mm(bk(7, 0, 128, 0, KVL),
                       xln[:, k * TOK + gto: k * TOK + gto + 128],
                       wdkvT_s[:, k * 128:(k + 1) * 128], BK[7],
                       ibufs=[LNOUT[k], SG[0], SG[1]], start=(k == 0),
                       stop=(k == DCH - 1),
                       inc=(k == DCH - 1))
                vop(lambda v, s=sl: v.tensor_copy(
                    stg_s[:, s * 128:(s + 1) * 128], bk(7, 0, 128, 0, KVL)),
                    obufs=[STGB[sl]], ibufs=[BK[7]])
                dma("sy", cc_in[gto:gto + 128, :], stg_s[:, sl * 128:(sl + 1) * 128],
                    obufs=[CCIN], ibufs=[STGB[sl]], group=True, sem="lms")
        cctok = P.op("gp", lambda g: g.collective_compute(
            "AllGather", ALU.bypass, replica_groups=[[0, 1, 2, 3], [4, 5, 6, 7]],
            ins=[cc_in.ap().opt()], outs=[cc_out.ap().opt()]),
            deps=CCIN.wdeps() + CCIN.rdeps(), inc="cc")
        CCOUT = buf("ccout")
        CCOUT.note_w(cctok)
        CKVM.begin_group()
        for k in range(T // 128):
            dma("sy", ckvM[:, k * 128:(k + 1) * 128],
                cc_out[gather_chunk_src(k):gather_chunk_src(k) + 128, :],
                obufs=[CKVM], ibufs=[CCOUT], group=True, sem="lmk")
        CKVT.begin_group()
        for k in range(T // 128):
            bi = 6 if (k % 2 == 0) else 7
            mm(bk(bi, 0, 128, 0, 128), ckvM[:, k * 128:(k + 1) * 128], identb_s[:, :],
               BK[bi], ibufs=[CKVM, cb["identb"]])
            vop(lambda v, k=k, bi=bi: v.tensor_copy(
                ckvT[:, k * 128:(k + 1) * 128], bk(bi, 0, 128, 0, 128)),
                obufs=[CKVT], ibufs=[BK[bi]], group=True)

    LNST = [LNOUT[0] for _ in range(DCH)]

    def emit_attn():
        if dbg_state.get("ckvm") is not None:
            dma("sy", dbg_state["ckvm"], ckvM[:, :], ibufs=[CKVM], sem="st")
            dma("sy", dbg_state["ckvt"], ckvT[:, :], ibufs=[CKVT], sem="st")
            dbg_state["ckvm"] = None
        for sgb in SG:
            sgb.begin_group()
        for k in range(DCH):
            dma("gp", wdqT_s[:, k * 128:(k + 1) * 128], dr["wdqT"][k],
                obufs=[SG[0], SG[1]], group=True, sem="lmx2")
            dma("gp", wuqT_s[:, k * 128:(k + 1) * 128],
                dr["wuqT"][:, k * 128:(k + 1) * 128], obufs=[SG[2], SG[3]], group=True,
                sem="lmx2")
        MEO = [EO[(mq, tq)] for mq in range(4) for tq in range(2)]
        for mb in MEO:
            mb.begin_group()
        dma("sy", masks_s[:, :], dr["masks"], obufs=MEO, group=True, sem="lmm")
        CKQ = LNOUT[0]
        C2 = buf("c2")
        CN = buf("cnorm")
        CN.begin_group()
        for piece in range(8):
            po = piece * 1024
            vop(lambda v, po=po: v.tensor_tensor(
                out=xln[:, 0:1024], in0=ckvT[:, po:po + 1024],
                in1=ckvT[:, po:po + 1024], op=ALU.mult), obufs=[CKQ], ibufs=[CKVT])
            for c in range(2):
                mm(bk(7, 0, 1, 0, 512), onescb_s[:, :],
                   xln[:, c * 512:(c + 1) * 512], BK[7], ibufs=[cb["onescb"], CKQ])
                vop(lambda v, piece=piece, c=c: v.tensor_reduce(
                    out=rows[0:1, piece * 2 + c:piece * 2 + c + 1],
                    in_=bk(7, 0, 1, 0, 512), axis=AX.X, op=ALU.max), obufs=[CN],
                    ibufs=[BK[7]], group=True)
        vop(lambda v: v.tensor_reduce(out=rows[0:1, 16:17], in_=rows[0:1, 0:16],
                                      axis=AX.X, op=ALU.max), obufs=[C2], ibufs=[CN])

        STGB2 = [buf("stg0"), buf("stg1")]
        QA, QT = buf("qa"), buf("cwqt")
        QL = [buf("ql0"), buf("ql1")]
        NB = [buf("nb0"), buf("nb1")]
        ZA = buf("zacc")
        CTXN = [buf("ctxn0"), buf("ctxn1")]
        PRB = [(EO[(4, 0)], EO[(4, 1)]), (EO[(5, 0)], EO[(5, 1)])]
        SCC = {n: buf("sc_" + n) for n in ("m2", "b2", "ln", "B", "nb")}
        ZR = buf("zrow")
        RZ = buf("rz")
        RZB = buf("rzb")

        for j in range(NST):
            so = j * STQ
            nch = 8 * (j + 1)
            ngr = nch // 4
            emit_ln(so, STQ, lnst_s, STQ, 0, 7, 6, 6, LNST)
            for k in range(DCH):
                mm(bk(7, 0, 128, 0, STQ), wdqT_s[:, k * 128:(k + 1) * 128],
                   lnst_s[:, k * STQ:(k + 1) * STQ], BK[7],
                   ibufs=[SG[0], SG[1], LNST[k]],
                   start=(k == 0), stop=(k == DCH - 1), inc=(k == DCH - 1))
            vop(lambda v: v.tensor_copy(qa_s[:, :], bk(7, 0, 128, 0, STQ)), obufs=[QA],
                ibufs=[BK[7]])
            QT.begin_group()
            for r in range(DCH):
                mm(bk(7, 0, 128, 0, STQ), wuqT_s[:, r * 128:(r + 1) * 128], qa_s[:, :],
                   BK[7], ibufs=[SG[2], SG[3], QA])
                vop(lambda v, r=r: v.tensor_copy(
                    qTst_s[:, r * STQ:(r + 1) * STQ], bk(7, 0, 128, 0, STQ)),
                    obufs=[QT], ibufs=[BK[7]], group=True)
            if dbg_state.get("qt") is not None and j == 0:
                dma("sy", dbg_state["keff"], keff_s[:, :], ibufs=[KEFF], sem="st")
                dma("sy", dbg_state["qa"], qa_s[:, :], ibufs=[QA], sem="st")
                dma("sy", dbg_state["qt"], qTst_s[:, :], ibufs=[QT], sem="st")
                dma("sy", dbg_state["lnst"], lnst_s[:, :],
                    ibufs=[LNST[0]], sem="st")
                dbg_state["qt"] = None
            def pair_prologue(pp):
                sl = pp % 2
                for hi in range(2):
                    h = 2 * pp + hi
                    lhs = keff_s[(h % 2) * 64:(h % 2) * 64 + 64,
                                 (h // 2) * 128:(h // 2) * 128 + 128]
                    rhs = qTst_s[(h % 2) * 64:(h % 2) * 64 + 64,
                                 (h // 2) * STQ:(h // 2) * STQ + STQ]
                    bki = 7 if hi == 0 else 6
                    mm(bk(bki, 0, 128, 0, STQ), lhs, rhs, BK[bki],
                       ibufs=[KEFF, QT], start=True, stop=True)
                vop(lambda v, sl=sl: v.tensor_copy(
                    qlT_s[:, sl * 512:sl * 512 + STQ], bk(7, 0, 128, 0, STQ)),
                    obufs=[QL[sl]], ibufs=[BK[7]])
                vop(lambda v, sl=sl: v.tensor_copy(
                    qlT_s[:, sl * 512 + STQ:sl * 512 + 512], bk(6, 0, 128, 0, STQ)),
                    obufs=[QL[sl]], ibufs=[BK[6]])
                if attn_level == 11:
                    return
                vop(lambda v, sl=sl: v.tensor_tensor(
                    out=stg_s[:, 0:256], in0=qlT_s[:, sl * 512:sl * 512 + 256],
                    in1=qlT_s[:, sl * 512:sl * 512 + 256], op=ALU.mult),
                    obufs=[STGB2[0], STGB2[1]], ibufs=[QL[sl]])
                mm(bk(7, 0, 1, 0, 256), onescb_s[:, :], stg_s[:, 0:256], BK[7],
                   ibufs=[cb["onescb"], STGB2[0], STGB2[1]])
                vop(lambda v, sl=sl: v.tensor_tensor(
                    out=stg_s[:, 0:256], in0=qlT_s[:, sl * 512 + 256:sl * 512 + 512],
                    in1=qlT_s[:, sl * 512 + 256:sl * 512 + 512], op=ALU.mult),
                    obufs=[STGB2[0], STGB2[1]], ibufs=[QL[sl], BK[7]])
                mm(bk(7, 0, 1, 256, 512), onescb_s[:, :], stg_s[:, 0:256], BK[7],
                   ibufs=[cb["onescb"], STGB2[0], STGB2[1]], start=True, stop=True)
                if attn_level == 12:
                    return
                vop(lambda v: v.tensor_reduce(
                    out=sc_s[0:1, 0:1], in_=bk(7, 0, 1, 0, 512), axis=AX.X, op=ALU.max),
                    obufs=[SCC["m2"]], ibufs=[BK[7]])
                vop(lambda v: v.tensor_tensor(out=sc_s[0:1, 1:2], in0=sc_s[0:1, 0:1],
                                              in1=rows[0:1, 16:17], op=ALU.mult),
                    obufs=[SCC["b2"]], ibufs=[SCC["m2"], C2])
                vop(lambda s: s.activation(out=sc_s[0:1, 2:3], in_=sc_s[0:1, 1:2],
                                           func=AF.Ln), obufs=[SCC["ln"]],
                    ibufs=[SCC["b2"]], eng="ac")
                vop(lambda s: s.activation(out=sc_s[0:1, 3:4], in_=sc_s[0:1, 2:3],
                                           func=AF.Exp, scale=0.5), obufs=[SCC["B"]],
                    ibufs=[SCC["ln"]], eng="ac")
                vop(lambda v: v.tensor_scalar(out=sc_s[0:1, 4:5], in0=sc_s[0:1, 3:4],
                                              scalar1=-1.0, scalar2=50.0, op0=ALU.mult,
                                              op1=ALU.add), obufs=[SCC["nb"]],
                    ibufs=[SCC["B"]])
                mm(bk(7, 0, 128, 300, 301), onesr_s[:, :], sc_s[0:1, 4:5], BK[7],
                   ibufs=[cb["onesr"], SCC["nb"]])
                vop(lambda v, sl=sl: v.tensor_copy(
                    negb_s[:, sl:sl + 1], bk(7, 0, 128, 300, 301)), obufs=[NB[sl]],
                    ibufs=[BK[7]])

            if attn_level >= 1:
                pair_prologue(0)
            for pp in range(NH // 2):
                if attn_level < 1:
                    break
                sl = pp % 2
                cbank = 4 + sl
                ngr = nch // 2

                def scores_group(g):
                    pair = g % 2
                    for q2 in range(2):
                        c = g * 2 + q2
                        hw = 2 * pair + q2
                        masked = (c >= nch - 8)
                        if masked:
                            mi = c - (nch - 8)
                            mm(bk(hw, 0, 128, 0, 512), identb_s[:, :],
                               masks_s[:, mi * 512:(mi + 1) * 512], BK[hw],
                               ibufs=[cb["identb"], MEO[mi]], start=True,
                               stop=False, inc=False)
                            mm(bk(hw, 0, 128, 0, 512),
                               ckvT[:, c * 128:(c + 1) * 128],
                               qlT_s[:, sl * 512:(sl + 1) * 512], BK[hw],
                               ibufs=[CKVT, QL[sl]], start=False, stop=True)
                        else:
                            mm(bk(hw, 0, 128, 0, 512),
                               ckvT[:, c * 128:(c + 1) * 128],
                               qlT_s[:, sl * 512:(sl + 1) * 512], BK[hw],
                               ibufs=[CKVT, QL[sl]], start=True, stop=True)

                def exp_g(g):
                    pair = g % 2
                    po = pair * 1024
                    vop(lambda s, po=po, pair=pair, sl=sl: s.activation(
                        out=probs_s[:, po:po + 1024], in_=PP[pair][:, :], func=AF.Exp,
                        bias=negb_s[:, sl:sl + 1], scale=1.0),
                        obufs=[PRB[pair][0], PRB[pair][1]],
                        ibufs=[BK[2 * pair], BK[2 * pair + 1], NB[sl]], eng="ac")

                zbank = 4 + (1 - sl)

                def zacc_g(g):
                    pair = g % 2
                    po = pair * 1024
                    for q2 in range(2):
                        c = g * 2 + q2
                        mm(bk(zbank, 0, 1, 0, 512), onescb_s[:, :],
                           probs_s[:, po + q2 * 512:po + (q2 + 1) * 512], BK[zbank],
                           ibufs=[cb["onescb"], PRB[pair][q2]], start=(c == 0),
                           stop=(c == nch - 1), inc=(c == nch - 1) or (q2 == 1))

                def ctx_g(g):
                    pair = g % 2
                    po = pair * 1024
                    for q2 in range(2):
                        c = g * 2 + q2
                        mm(bk(cbank, 0, 128, 0, 512), ckvM[:, c * 128:(c + 1) * 128],
                           probs_s[:, po + q2 * 512:po + (q2 + 1) * 512], BK[cbank],
                           ibufs=[CKVM, PRB[pair][q2]], start=(c == 0),
                           stop=(c == nch - 1), inc=(c == nch - 1) or (q2 == 1))

                if attn_level >= 2:
                    scores_group(0)
                    for g in range(ngr):
                        if attn_level >= 3:
                            exp_g(g)
                        if attn_level >= 4:
                            zacc_g(g)
                        if g + 1 < ngr:
                            scores_group(g + 1)
                        if attn_level >= 5:
                            ctx_g(g)
                if pp + 1 < NH // 2 and attn_level >= 1:
                    pair_prologue(pp + 1)
                if attn_level < 5:
                    continue
                vop(lambda v, zb=zbank: v.tensor_copy(
                    zr_s[0:1, 0:512], bk(zb, 0, 1, 0, 512)), obufs=[ZR],
                    ibufs=[BK[zbank]])
                vop(lambda v: v.tensor_scalar(
                    out=zr_s[0:1, 0:512], in0=zr_s[0:1, 0:512], scalar1=1e-30,
                    scalar2=None, op0=ALU.add), obufs=[ZR], ibufs=[ZR])
                vop(lambda v: v.reciprocal(zr_s[0:1, 512:1024], zr_s[0:1, 0:512]),
                    obufs=[RZ], ibufs=[ZR])
                mm(bk(7, 0, 128, 0, 512), onesr_s[:, :], zr_s[0:1, 512:1024],
                   BK[7], ibufs=[cb["onesr"], RZ])
                vop(lambda v: v.tensor_copy(rzb_s[:, :], bk(7, 0, 128, 0, 512)),
                    obufs=[RZB], ibufs=[BK[7]])
                vop(lambda v, cbk=cbank: v.tensor_tensor(
                    out=ctxn_s[:, 0:512], in0=bk(cbk, 0, 128, 0, 512),
                    in1=rzb_s[:, :], op=ALU.mult), obufs=[CTXN[0]],
                    ibufs=[BK[cbank], RZB])
                h0, h1 = 2 * pp, 2 * pp + 1
                mm(bk(6, 0, 64, 0, STQ), veff_s[:, h0 * 64:h0 * 64 + 64],
                   ctxn_s[:, 0:STQ], BK[6], ibufs=[VEFF, CTXN[0]],
                   start=True, stop=True, inc=False)
                mm(bk(6, 64, 128, 0, STQ), veff_s[:, h1 * 64:h1 * 64 + 64],
                   ctxn_s[:, STQ:2 * STQ], BK[6], ibufs=[VEFF, CTXN[0]],
                   start=True, stop=True, tile_position=(0, 64))
                dchunk = pp
                vop(lambda v, dc=dchunk, so=so: v.tensor_tensor(
                    out=resid[:, dc * TOK + so: dc * TOK + so + STQ],
                    in0=resid[:, dc * TOK + so: dc * TOK + so + STQ],
                    in1=bk(6, 0, 128, 0, STQ), op=ALU.add),
                    obufs=[RES[dchunk]], ibufs=[BK[6], RES[dchunk]])

    dbg_outs = []
    if debug:
        dbg_xln = nc.dram_tensor("dbg_xln", [128, DCH * TOK], BF16,
                                 kind="ExternalOutput").ap()
        dbg_cw8 = nc.dram_tensor("dbg_cw8", [E, TOK], BF16,
                                 kind="ExternalOutput").ap()
        dbg_eo = nc.dram_tensor("dbg_eo", [128, HCH * 1024], BF16,
                                 kind="ExternalOutput").ap()
        dbg_state["eo"] = dbg_eo
        dbg_w = nc.dram_tensor("dbg_w", [128, 2048], BF16, kind="ExternalOutput").ap()
        dbg_state["w"] = dbg_w
        dbg_sg = nc.dram_tensor("dbg_sg", [128, 1024], BF16, kind="ExternalOutput").ap()
        dbg_state["sg"] = dbg_sg
        dbg_ckvm = nc.dram_tensor("dbg_ckvm", [128, T], BF16, kind="ExternalOutput").ap()
        dbg_state["ckvm"] = dbg_ckvm
        dbg_ckvt = nc.dram_tensor("dbg_ckvt", [128, T], BF16, kind="ExternalOutput").ap()
        dbg_state["ckvt"] = dbg_ckvt
        dbg_nb = nc.dram_tensor("dbg_nb", [128, 2], F32, kind="ExternalOutput").ap()
        dbg_state["nb"] = dbg_nb
        dbg_rows = nc.dram_tensor("dbg_rows", [1, 64], F32, kind="ExternalOutput").ap()
        dbg_state["rows"] = dbg_rows
        dbg_sc = nc.dram_tensor("dbg_sc", [1, 16], F32, kind="ExternalOutput").ap()
        dbg_state["sc"] = dbg_sc
        dbg_ql = nc.dram_tensor("dbg_ql", [128, 2 * STQ], BF16, kind="ExternalOutput").ap()
        dbg_state["ql"] = dbg_ql
        dbg_keff = nc.dram_tensor("dbg_keff", [128, DCH * 128], BF16, kind="ExternalOutput").ap()
        dbg_state["keff"] = dbg_keff
        dbg_qa = nc.dram_tensor("dbg_qa", [128, STQ], BF16, kind="ExternalOutput").ap()
        dbg_state["qa"] = dbg_qa
        dbg_qt = nc.dram_tensor("dbg_qt", [128, DCH * STQ], BF16, kind="ExternalOutput").ap()
        dbg_state["qt"] = dbg_qt
        dbg_lnst = nc.dram_tensor("dbg_lnst", [128, DCH * STQ], BF16, kind="ExternalOutput").ap()
        dbg_state["lnst"] = dbg_lnst

    if "ffn1" in phases:
        emit_ffn()
        if debug:
            dma("sy", dbg_xln, xln[:, :], ibufs=[LNOUT[k] for k in range(DCH)],
                sem="st")
            dma("sy", dbg_cw8, cw8[:, :], ibufs=[CW8], sem="st")
    if "p2" in phases:
        emit_p2()
    if "attn" in phases:
        emit_attn()
    if "ffn2" in phases:
        emit_ffn()

    OUTB = buf("out")
    OUTB.begin_group()
    for k in range(DCH):
        dma("sy", out_d[k * 128:(k + 1) * 128, :], resid[:, k * TOK:(k + 1) * TOK],
            obufs=[OUTB], ibufs=[RES[k]], sem="st", group=True)

    semnames = ("lm0", "lmo", "lms", "lmk", "lmm", "lmx0", "lmx1", "lmx2",
            "wca", "wcb", "w2a", "w2b", "w2c", "w2d", "pe", "ve", "ac", "cc", "st")
    with ExitStack() as ses:
        SEM = {n: ses.enter_context(nc.semaphore("s_" + n)) for n in semnames}
        block = ses.enter_context(nc.Block())

        def run(engine_obj, qname):
            for build, waits, inc, amt in P.q[qname]:
                for s, v in waits:
                    engine_obj.wait_ge(SEM[s], v)
                inst = build(engine_obj)
                if inc is not None:
                    inst.then_inc(SEM[inc], amt)

        @block.tensor
        def _(t):
            run(t, "pe")

        @block.vector
        def _(v):
            run(v, "ve")

        @block.scalar
        def _(s):
            run(s, "ac")

        @block.gpsimd
        def _(g):
            run(g, "gp")

        @block.sync
        def _(sy):
            run(sy, "sy")
            sy.wait_ge(SEM["st"], P.ctr["st"])

    es.close()
    return nc


_prog_cache = {}
PHASES = ("ffn1", "p2", "attn", "ffn2")
DEBUG = False
LAST_RES = {}


EXPERTS = tuple(range(NEXP))


ATTN_LEVEL = 5


def _get_prog():
    key = (PHASES, DEBUG, EXPERTS, ATTN_LEVEL)
    if key not in _prog_cache:
        _prog_cache[key] = build_program(PHASES, DEBUG, EXPERTS, ATTN_LEVEL)
    return _prog_cache[key]


def kernel(L0_attn_x, L0_attn_W_dq, L0_attn_W_uq, L0_attn_W_dkv, L0_attn_W_uk,
           L0_attn_W_uv, L0_attn_W_o, L0_ffn_router_w, L0_ffn_experts_bias,
           L0_ffn_w1, L0_ffn_w3, L0_ffn_w2, L0_ffn_sw1, L0_ffn_sw3, L0_ffn_sw2):
    f = lambda a: np.ascontiguousarray(np.asarray(a, dtype=np.float32))
    x = f(L0_attn_x)
    w1a = np.concatenate([f(L0_ffn_w1), f(L0_ffn_sw1)], axis=0)
    w3a = np.concatenate([f(L0_ffn_w3), f(L0_ffn_sw3)], axis=0)
    w2a = np.concatenate([f(L0_ffn_w2), f(L0_ffn_sw2)], axis=0)

    def mk(w):
        r = w.reshape(NEXP, DCH, 128, HCH, 128)
        return r.transpose(0, 3, 2, 1, 4).reshape(NEXP, HCH, 128, DCH * 128)

    wcat = np.ascontiguousarray(np.concatenate([mk(w1a), mk(w3a)], axis=3))
    w2cat = np.ascontiguousarray(w2a.reshape(NEXP, HCH, 128, D))

    wdq = f(L0_attn_W_dq)
    wdqT = np.ascontiguousarray(wdq.T.reshape(DCH, 128, 128))
    wuqT = np.ascontiguousarray(f(L0_attn_W_uq).T)
    wdkvT = np.ascontiguousarray(f(L0_attn_W_dkv).T.reshape(DCH, 128, 128))
    wukr = np.ascontiguousarray(f(L0_attn_W_uk).reshape(DCH, 128, KVL))
    wuvr = np.ascontiguousarray(f(L0_attn_W_uv).reshape(DCH, 128, KVL))
    woT = np.ascontiguousarray(f(L0_attn_W_o).T.reshape(DCH, 128, D))
    routT = np.ascontiguousarray(f(L0_ffn_router_w).T.reshape(DCH, 128, E))
    bias8 = np.ascontiguousarray(f(L0_ffn_experts_bias).reshape(E, 1))

    identb = np.eye(128, dtype=np.float32).astype(bfnp)
    identf = np.eye(128, dtype=np.float32)
    onesr = np.ones((1, 128), np.float32)
    onescb = np.ones((128, 1), np.float32).astype(bfnp)
    onescf = np.ones((128, 1), np.float32)
    onehot = np.zeros((E, E * 128), np.float32)
    for e in range(E):
        onehot[e, e * 128:(e + 1) * 128] = 1.0
    onehot = onehot.astype(bfnp)

    pos = _local_positions()
    in_maps = []
    for core in range(NCORE):
        b = core // 4
        ct = core % 4
        xT = np.ascontiguousarray(x[b][pos[ct], :].T)
        in_maps.append(dict(
            xT=xT, wcat=wcat, w2cat=w2cat, wdqT=wdqT, wuqT=wuqT, wdq=wdq,
            wdkvT=wdkvT, wuk=wukr, wuv=wuvr, woT=woT, routT=routT, bias8=bias8,
            masks=make_masks(ct), identb=identb, identf=identf, onesr=onesr,
            onescb=onescb, onescf=onescf, onehot=onehot,
        ))

    nc = _get_prog()
    res = run_bass_kernel_spmd(nc, in_maps, core_ids=list(range(NCORE)))
    LAST_RES["res"] = res

    out = np.empty((B, T, C), dtype=np.float32)
    for core in range(NCORE):
        b = core // 4
        ct = core % 4
        out[b][pos[ct], :] = res.results[core]["outT"].T
    return out


# version 06 (final)
# speedup vs baseline: 1.0715x; 1.0715x over previous
import numpy as np
import ml_dtypes
from contextlib import ExitStack

import concourse.bass as bass
import concourse.mybir as mybir
from concourse.bass_utils import run_bass_kernel_spmd

F32 = mybir.dt.float32
BF16 = mybir.dt.bfloat16
AF = mybir.ActivationFunctionType
ALU = mybir.AluOpType
AX = mybir.AxisListType

B, T, C = 2, 8192, 1024
NH, HS, KVL = 16, 64, 128
D, H, E = 1024, 2048, 8
NCORE = 8
NST = 8
STQ = 256
TOK = NST * STQ
TAU = 512
NTAU = TOK // TAU
DCH = D // 128
HCH = H // 128
NEXP = 9
EPS = 1e-5

bfnp = ml_dtypes.bfloat16
ENGINES = ("pe", "ve", "ac", "gp", "sy")


class Buf:
    __slots__ = ("w", "r", "snap")

    def __init__(self):
        self.w = []
        self.r = []
        self.snap = None

    def wdeps(self):
        return self.r + self.w

    def rdeps(self):
        return list(self.w)

    def note_w(self, tok):
        if tok is not None:
            self.w = [tok]
            self.r = []
            self.snap = None

    def begin_group(self):
        self.snap = self.wdeps()
        self.w = []
        self.r = []

    def note_w_group(self, tok):
        if tok is not None:
            self.w.append(tok)

    def note_r(self, tok):
        if tok is not None:
            self.r.append(tok)


class Prog:
    def __init__(self, semnames):
        self.q = {e: [] for e in ENGINES}
        self.ctr = {s: 0 for s in semnames}
        self.waited = {e: {s: 0 for s in semnames} for e in ENGINES}

    BURST = ("lm0", "lmo", "lms", "lmk", "lmm", "lmx0", "lmx1", "lmx2")

    def op(self, eng, build, deps=(), inc=None, amt=1):
        waits = []
        for d in deps:
            if d is None:
                continue
            s, v, peng = d
            if eng == "pe" and peng == "pe":
                continue
            if s in self.BURST:
                v = self.ctr[s]
            if v > self.waited[eng][s]:
                self.waited[eng][s] = v
                waits.append((s, v))
        tok = None
        if inc is not None:
            self.ctr[inc] += amt
            tok = (inc, self.ctr[inc], eng)
        self.q[eng].append((build, waits, inc, amt))
        return tok


def _local_positions():
    pos = {}
    for ct in range(4):
        j = np.arange(NST)[:, None]
        q = np.arange(STQ)[None, :]
        pos[ct] = (1024 * j + 256 * ct + q).reshape(-1)
    return pos


def make_masks(ct):
    s = np.arange(128)
    q = np.arange(STQ)
    out = np.zeros((128, 16 * STQ), dtype=np.float32)
    for c in range(8):
        keyr = 128 * c + s[:, None]
        qg = 256 * ct + q[None, :]
        m = np.where(keyr <= qg, 0.0, -1e30)
        out[:, 2 * c * STQ:(2 * c + 1) * STQ] = m
        out[:, (2 * c + 1) * STQ:(2 * c + 2) * STQ] = m
    return np.ascontiguousarray(out.astype(bfnp))


def gather_chunk_src(k):
    ct = (k % 8) // 2
    j = k // 8
    return ct * TOK + j * STQ + (k % 2) * 128


def build_program(phases=("ffn1", "p2", "attn", "ffn2"), debug=False, experts=tuple(range(NEXP)), attn_level=5):
    nc = bass.Bass("TRN2", target_bir_lowering=False)
    dr = {}

    def din(name, shape, dt=F32):
        dr[name] = nc.dram_tensor(name, list(shape), dt, kind="ExternalInput").ap()
        return dr[name]

    din("xT", (D, TOK))
    din("wcat", (NEXP, HCH, 128, 2048))
    din("w2cat", (NEXP, HCH, 128, D))
    din("wdqT", (DCH, 128, 128))
    din("wuqT", (128, D))
    din("wdq", (128, D))
    din("wdkvT", (DCH, 128, 128))
    din("wuk", (DCH, 128, KVL))
    din("wuv", (DCH, 128, KVL))
    din("woT", (DCH, 128, D))
    din("routT", (DCH, 128, E))
    din("bias8", (E, 1))
    din("masks", (128, 16 * STQ), BF16)
    din("identb", (128, 128), BF16)
    din("identf", (128, 128))
    din("onesr", (1, 128))
    din("onescb", (128, 1), BF16)
    din("onescf", (128, 1))
    din("onehot", (E, E * 128), BF16)
    out_d = nc.dram_tensor("outT", [D, TOK], F32, kind="ExternalOutput").ap()
    cc_in = nc.dram_tensor("cc_in", [TOK, KVL], BF16, kind="Internal")
    cc_out = nc.dram_tensor("cc_out", [4 * TOK, KVL], BF16, kind="Internal")

    es = ExitStack()
    S = lambda n, sh, dt: es.enter_context(nc.sbuf_tensor(n, list(sh), dt))

    resid = S("resid", (128, DCH * TOK), F32)
    xln = S("xln", (128, DCH * TOK), BF16)
    eo = S("eo", (128, HCH * 1024), BF16)
    ckvT = S("ckvT", (128, T), BF16)
    ckvM = S("ckvM", (128, T), BF16)
    wcat_s = S("wcat_s", (128, 2 * 2048), BF16)
    w2b_s = S("w2b_s", (128, 4 * 512), BF16)
    cwb_s = S("cwb_s", (128, 2 * TAU), BF16)
    keff_s = S("keff_s", (128, DCH * 128), BF16)
    veff_s = S("veff_s", (128, D), BF16)
    routT_s = S("routT_s", (128, DCH * E), BF16)
    bias8_s = S("bias8_s", (E, 1), F32)
    identb_s = S("identb_s", (128, 128), BF16)
    identf_s = S("identf_s", (128, 128), F32)
    onesr_s = S("onesr_s", (1, 128), F32)
    onescb_s = S("onescb_s", (128, 1), BF16)
    onescf_s = S("onescf_s", (128, 1), F32)
    t2_s = S("t2_s", (128, 48), F32)
    sq_s = S("sq_s", (128, TAU), F32)
    sgub = S("sgub", (128, 4 * TAU), BF16)
    stat = S("stat", (1, 3 * TAU), F32)
    zr_s = S("zr_s", (1, 1024), F32)
    rows = S("rows", (1, 24), F32)
    sc_s = S("sc_s", (1, 16), F32)
    zacc_s = S("zacc_s", (128, 2 * STQ), F32)
    qTst_s = S("qTst_s", (128, DCH * STQ), BF16)
    qlT_s = S("qlT_s", (128, 4 * STQ), BF16)
    qa_s = S("qa_s", (128, STQ), BF16)
    rzb_s = S("rzb_s", (128, 2 * STQ), F32)
    negb_s = S("negb_s", (128, 2), F32)
    ctxn_s = S("ctxn_s", (128, 2 * STQ), BF16)
    stg_s = S("stg_s", (128, 2 * 128), BF16)
    wdqT_s = sgub[:, 0:1024]
    wuqT_s = sgub[:, 1024:2048]
    wdkvT_s = sgub[:, 0:1024]
    lnst_s = xln[:, 0:DCH * STQ]
    cw8 = qTst_s[0:E, 0:TOK]
    l8_s = sq_s[0:E, 0:TAU]
    masks_s = eo[:, 0:16 * STQ]
    probs_s = eo[:, 16 * STQ:24 * STQ]
    onehot_s = ckvM[0:E, 0:E * 128]

    PP = [es.enter_context(nc.psum_tensor(f"pp{i}", [128, 1024], F32))
          for i in range(4)]

    def bk(i, p0=0, p1=128, f0=0, f1=512):
        return PP[i // 2][p0:p1, (i % 2) * 512 + f0:(i % 2) * 512 + f1]

    sems = ("lm0", "lmo", "lms", "lmk", "lmm", "lmx0", "lmx1", "lmx2",
            "wca", "wcb", "w2a", "w2b", "w2c", "w2d", "pe", "ve", "ac", "cc", "st")
    P = Prog(sems)
    bufs = {}

    def buf(name):
        if name not in bufs:
            bufs[name] = Buf()
        return bufs[name]

    BK = [buf(f"bank_{i}") for i in range(8)]
    RES = [buf(f"resid_{k}") for k in range(DCH)]

    def dma(queue, dst_ap, src_ap, obufs=(), ibufs=(), sem=None, group=False):
        assert sem is not None
        deps = []
        for ob in obufs:
            deps += (ob.snap if (group and ob.snap is not None) else ob.wdeps())
        for ib in ibufs:
            deps += ib.rdeps()
        tok = P.op(queue, lambda g, d=dst_ap, s=src_ap: g.dma_start(out=d, in_=s),
                   deps=deps, inc=sem, amt=16)
        for ob in obufs:
            (ob.note_w_group if group else ob.note_w)(tok)
        for ib in ibufs:
            ib.note_r(tok)
        return tok

    def mm(out_ap, lhsT_ap, rhs_ap, obank, ibufs=(), start=True, stop=True, inc=True,
           tile_position=None):
        deps = []
        if start:
            deps += obank.wdeps()
        for ib in ibufs:
            deps += ib.rdeps()

        def b(t, o=out_ap, l=lhsT_ap, r=rhs_ap, st=start, sp=stop, tp=tile_position):
            if tp is None:
                return t.matmul(o, l, r, start=st, stop=sp)
            return t.matmul(o, l, r, start=st, stop=sp, tile_position=tp)

        tok = P.op("pe", b, deps=deps, inc=("pe" if inc else None))
        if stop and inc:
            obank.note_w(tok)
        for ib in ibufs:
            ib.note_r(tok)
        return tok

    def vop(build, obufs=(), ibufs=(), eng="ve", group=False):
        sem = "ve" if eng == "ve" else "ac"
        deps = []
        for ob in obufs:
            deps += (ob.snap if (group and ob.snap is not None) else ob.wdeps())
        for ib in ibufs:
            deps += ib.rdeps()
        tok = P.op(eng, build, deps=deps, inc=sem)
        for ob in obufs:
            (ob.note_w_group if group else ob.note_w)(tok)
        for ib in ibufs:
            ib.note_r(tok)
        return tok

    cb = {}
    for name, tile in [("identb", identb_s), ("identf", identf_s),
                       ("onesr", onesr_s), ("onescb", onescb_s), ("onescf", onescf_s),
                       ("bias8", bias8_s)]:
        cb[name] = buf("c_" + name)
        dma("sy", tile[:, :], dr[name], obufs=[cb[name]], sem="lm0")
    ROUTT = buf("routT")
    ROUTT.begin_group()
    for k in range(DCH):
        dma("gp", routT_s[:, k * E:(k + 1) * E], dr["routT"][k], obufs=[ROUTT],
            group=True, sem="lmx0")
    for k in range(DCH):
        dma("sy", resid[:, k * TOK:(k + 1) * TOK], dr["xT"][k * 128:(k + 1) * 128, :],
            obufs=[RES[k]], sem="lm0")

    CKVT = buf("ckvT")
    CKVM = buf("ckvM")
    LNOUT = [buf(f"lnout{k}") for k in range(DCH)]
    SG = [buf(f"sg{i}") for i in range(4)]
    EO = {(m, ti): buf(f"eo{m}_{ti}") for m in range(HCH) for ti in range(2)}

    SG[0].begin_group()
    SG[1].begin_group()
    SG[2].begin_group()
    SG[3].begin_group()
    for k in range(DCH):
        dma("gp", sgub[:, k * 128:(k + 1) * 128], dr["wuk"][k], obufs=[SG[0], SG[1]],
            group=True, sem="lmx0")
        dma("gp", wuqT_s[:, k * 128:(k + 1) * 128],
            dr["wuqT"][:, k * 128:(k + 1) * 128], obufs=[SG[2], SG[3]], group=True,
            sem="lmx0")
    dma("gp", ckvT[:, 0:D], dr["wdq"], obufs=[CKVT], sem="lmx0")
    LNOUT[0].begin_group()
    for k in range(DCH):
        dma("gp", xln[:, k * 128:(k + 1) * 128], dr["wuv"][k], obufs=[LNOUT[0]],
            group=True, sem="lmx0")

    for c in range(DCH):
        for h2 in range(2):
            dma("gp", eo[:, 8192 + c * 1024 + h2 * 512: 8192 + c * 1024 + (h2 + 1) * 512],
                dr["woT"][c][:, h2 * 512:(h2 + 1) * 512], obufs=[EO[(8 + c, h2)]],
                group=False, sem="lmx0")

    for c in range(DCH):
        for half in range(2):
            i = half
            mm(bk(i), wuqT_s[:, c * 128:(c + 1) * 128],
               ckvT[:, half * 512:(half + 1) * 512], BK[i],
               ibufs=[SG[2], SG[3], CKVT])
            vop(lambda v, c=c, h=half, i=i: v.tensor_copy(
                eo[:, c * 1024 + h * 512:c * 1024 + (h + 1) * 512], bk(i)),
                obufs=[EO[(c, half)]], ibufs=[BK[i]])
    KEFF = buf("keff")
    KEFF.begin_group()
    for r in range(DCH):
        for c in range(DCH):
            mm(bk(2, f1=128), eo[:, c * 1024 + r * 128:c * 1024 + (r + 1) * 128],
               sgub[:, c * 128:(c + 1) * 128], BK[2],
               ibufs=[EO[(c, r // 4)], SG[0], SG[1]], start=(c == 0),
               stop=(c == DCH - 1), inc=(c == DCH - 1))
        vop(lambda v, r=r: v.tensor_scalar(
            out=keff_s[:, r * 128:(r + 1) * 128], in0=bk(2, f1=128),
            scalar1=0.125, scalar2=None, op0=ALU.mult), obufs=[KEFF], ibufs=[BK[2]],
            group=True)
    VEFF = buf("veff")
    VEFF.begin_group()
    for half in range(2):
        for c in range(DCH):
            wo_t = eo[:, 8192 + c * 1024 + half * 512: 8192 + c * 1024 + (half + 1) * 512]
            mm(bk(3), xln[:, c * 128:(c + 1) * 128], wo_t, BK[3],
               ibufs=[LNOUT[0], EO[(8 + c, half)]], start=(c == 0), stop=(c == DCH - 1),
               inc=(c == DCH - 1))
        vop(lambda v, h=half: v.tensor_copy(
            veff_s[:, h * 512:(h + 1) * 512], bk(3)), obufs=[VEFF], ibufs=[BK[3]],
            group=True)

    STATB = {n: buf("st_" + n) for n in
             ("mean", "var", "m2", "lnv", "rstd", "mr")}
    SQS = [buf("sqslot0")]
    def emit_ln(tok_off, ntok, dst, dst_chunk_stride, dst_off, ba, bb1_i, bb2_i, outb):
        for k in range(DCH):
            rs = resid[:, k * TOK + tok_off: k * TOK + tok_off + ntok]
            vop(lambda v, rs=rs, n=ntok: v.tensor_tensor(
                out=sq_s[:, 0:n], in0=rs, in1=rs, op=ALU.mult),
                obufs=[SQS[0]], ibufs=[RES[k]])
            mm(bk(ba, 0, 1, 0, ntok), onescf_s[:, :], rs, BK[ba],
               ibufs=[cb["onescf"], RES[k]], start=(k == 0), stop=(k == DCH - 1),
               inc=(k == DCH - 1))
            mm(bk(ba, 32, 33, 0, ntok), onescf_s[:, :], sq_s[:, 0:ntok],
               BK[ba], ibufs=[cb["onescf"], SQS[0]], start=(k == 0),
               stop=(k == DCH - 1), inc=True)
        vop(lambda v, n=ntok: v.tensor_scalar(
            out=stat[0:1, 0:n], in0=bk(ba, 0, 1, 0, n), scalar1=1.0 / D, scalar2=None,
            op0=ALU.mult), obufs=[STATB["mean"]], ibufs=[BK[ba]])
        vop(lambda v, n=ntok: v.tensor_scalar(
            out=stat[0:1, TAU:TAU + n], in0=bk(ba, 32, 33, 0, n), scalar1=1.0 / D,
            scalar2=EPS, op0=ALU.mult, op1=ALU.add), obufs=[STATB["var"]],
            ibufs=[BK[ba]])
        vop(lambda v, n=ntok: v.tensor_tensor(
            out=stat[0:1, 2 * TAU:2 * TAU + n], in0=stat[0:1, 0:n],
            in1=stat[0:1, 0:n], op=ALU.mult), obufs=[STATB["m2"]],
            ibufs=[STATB["mean"]])
        vop(lambda v, n=ntok: v.tensor_tensor(
            out=stat[0:1, TAU:TAU + n], in0=stat[0:1, TAU:TAU + n],
            in1=stat[0:1, 2 * TAU:2 * TAU + n], op=ALU.subtract),
            obufs=[STATB["var"]], ibufs=[STATB["m2"], STATB["var"]])
        vop(lambda s, n=ntok: s.activation(
            out=stat[0:1, 2 * TAU:2 * TAU + n], in_=stat[0:1, TAU:TAU + n],
            func=AF.Ln),
            obufs=[STATB["lnv"]], ibufs=[STATB["var"]], eng="ac")
        vop(lambda s, n=ntok: s.activation(
            out=stat[0:1, TAU:TAU + n], in_=stat[0:1, 2 * TAU:2 * TAU + n],
            func=AF.Exp, scale=-0.5), obufs=[STATB["rstd"]], ibufs=[STATB["lnv"]], eng="ac")
        vop(lambda v, n=ntok: v.tensor_tensor(
            out=stat[0:1, 2 * TAU:2 * TAU + n], in0=stat[0:1, 0:n],
            in1=stat[0:1, TAU:TAU + n], op=ALU.mult), obufs=[STATB["mr"]],
            ibufs=[STATB["mean"], STATB["rstd"]])
        vop(lambda v, n=ntok: v.tensor_scalar(
            out=stat[0:1, 2 * TAU:2 * TAU + n],
            in0=stat[0:1, 2 * TAU:2 * TAU + n],
            scalar1=-1.0, scalar2=None, op0=ALU.mult), obufs=[STATB["mr"]],
            ibufs=[STATB["mr"]])
        off2 = ntok if bb1_i == bb2_i else 0
        mm(bk(bb1_i, 0, 128, 0, ntok), onesr_s[:, :], stat[0:1, TAU:TAU + ntok],
           BK[bb1_i], ibufs=[cb["onesr"], STATB["rstd"]])
        mm(bk(bb2_i, 0, 128, off2, off2 + ntok), onesr_s[:, :],
           stat[0:1, 2 * TAU:2 * TAU + ntok], BK[bb2_i],
           ibufs=[cb["onesr"], STATB["mr"]])
        for k in range(DCH):
            rs = resid[:, k * TOK + tok_off: k * TOK + tok_off + ntok]
            vop(lambda v, rs=rs, n=ntok, b1=bb1_i: v.tensor_tensor(
                out=sq_s[:, 0:n], in0=rs, in1=bk(b1, 0, 128, 0, n),
                op=ALU.mult), obufs=[SQS[0]], ibufs=[BK[bb1_i], RES[k]])
            vop(lambda v, k=k, n=ntok, dst=dst, st=dst_chunk_stride, do=dst_off,
                b2=bb2_i, o2=off2: v.tensor_tensor(
                    out=dst[:, k * st + do: k * st + do + n],
                    in0=sq_s[:, 0:n], in1=bk(b2, 0, 128, o2, o2 + n),
                    op=ALU.add), obufs=[outb[k]], ibufs=[BK[bb2_i], SQS[0]])

    dbg_state = {}
    CW8 = buf("cwqt")
    T2 = buf("t2")
    L8 = SQS[0]
    CWB = [buf("cwb0"), buf("cwb1")]
    WCAT = [buf("wcat0"), buf("wcat1")]
    W2B = [buf(f"w2b{i}") for i in range(4)]

    def emit_ffn():
        CW8.begin_group()
        dma("sy", onehot_s, dr["onehot"], obufs=[CKVM], sem="lmo")
        for tau in range(NTAU):
            to = tau * TAU
            emit_ln(to, TAU, xln, TOK, to, 4, 5, 6, LNOUT)
            for k in range(DCH):
                mm(bk(7, 0, E, 0, TAU), routT_s[:, k * E:(k + 1) * E],
                   xln[:, k * TOK + to: k * TOK + to + TAU], BK[7],
                   ibufs=[ROUTT, LNOUT[k]], start=(k == 0), stop=(k == DCH - 1),
                   inc=(k == DCH - 1))
            vop(lambda v: v.tensor_scalar(
                out=l8_s[0:E, 0:TAU], in0=bk(7, 0, E, 0, TAU),
                scalar1=bias8_s[0:E, 0:1], scalar2=None, op0=ALU.add),
                obufs=[L8], ibufs=[BK[7], cb["bias8"]])
            for cc_ in range(TAU // 128):
                mm(bk(6, 0, 128, 0, E), l8_s[0:E, cc_ * 128:(cc_ + 1) * 128],
                   identf_s[0:E, 0:E], BK[6], ibufs=[L8, cb["identf"]])
                lt = t2_s[:, 0:8]
                m1 = t2_s[:, 8:9]
                m2 = t2_s[:, 9:10]
                dd = t2_s[:, 10:11]
                aa = t2_s[:, 11:12]
                bb_ = t2_s[:, 12:13]
                k1 = t2_s[:, 16:24]
                k2 = t2_s[:, 24:32]
                l2 = t2_s[:, 32:40]
                vop(lambda v: v.tensor_copy(lt, bk(6, 0, 128, 0, E)), obufs=[T2],
                    ibufs=[BK[6]])
                vop(lambda v: v.tensor_reduce(out=m1, in_=lt, axis=AX.X, op=ALU.max),
                    obufs=[T2], ibufs=[T2])
                vop(lambda v: v.tensor_scalar(out=k1, in0=lt, scalar1=m1, scalar2=None,
                                              op0=ALU.is_ge), obufs=[T2], ibufs=[T2])
                vop(lambda v: v.tensor_scalar(out=l2, in0=k1, scalar1=-1e30,
                                              scalar2=None, op0=ALU.mult),
                    obufs=[T2], ibufs=[T2])
                vop(lambda v: v.tensor_tensor(out=l2, in0=l2, in1=lt, op=ALU.add),
                    obufs=[T2], ibufs=[T2])
                vop(lambda v: v.tensor_reduce(out=m2, in_=l2, axis=AX.X, op=ALU.max),
                    obufs=[T2], ibufs=[T2])
                vop(lambda v: v.tensor_scalar(out=k2, in0=l2, scalar1=m2, scalar2=None,
                                              op0=ALU.is_ge), obufs=[T2], ibufs=[T2])
                vop(lambda v: v.tensor_tensor(out=dd, in0=m1, in1=m2, op=ALU.subtract),
                    obufs=[T2], ibufs=[T2])
                vop(lambda s: s.activation(out=aa, in_=dd, func=AF.Sigmoid),
                    obufs=[T2], ibufs=[T2], eng="ac")
                vop(lambda s: s.activation(out=bb_, in_=dd, func=AF.Sigmoid, scale=-1.0),
                    obufs=[T2], ibufs=[T2], eng="ac")
                vop(lambda v: v.tensor_scalar(out=k1, in0=k1, scalar1=aa, scalar2=None,
                                              op0=ALU.mult), obufs=[T2], ibufs=[T2])
                vop(lambda v: v.tensor_scalar(out=k2, in0=k2, scalar1=bb_, scalar2=None,
                                              op0=ALU.mult), obufs=[T2], ibufs=[T2])
                vop(lambda v: v.tensor_tensor(out=k1, in0=k1, in1=k2, op=ALU.add),
                    obufs=[T2], ibufs=[T2])
                vop(lambda v: v.tensor_scalar(out=k1, in0=k1, scalar1=0.5, scalar2=None,
                                              op0=ALU.mult), obufs=[T2], ibufs=[T2])
                mm(bk(6, 0, E, 128, 256), k1, identf_s[:, :], BK[6],
                   ibufs=[T2, cb["identf"]])
                vop(lambda v, to=to, cc_=cc_: v.tensor_copy(
                    cw8[0:E, to + cc_ * 128: to + (cc_ + 1) * 128],
                    bk(6, 0, E, 128, 256)), obufs=[CW8], ibufs=[BK[6]], group=True)

        for e in experts:
            for tp in range(2):
                if e < E:
                    for ti in range(2):
                        tau = 2 * tp + ti
                        mm(bk(5, 0, 128, 0, TAU), onehot_s[0:E, e * 128:(e + 1) * 128],
                           cw8[0:E, tau * TAU:(tau + 1) * TAU], BK[5],
                           ibufs=[CKVM, CW8])
                        vop(lambda v, ti=ti: v.tensor_copy(
                            cwb_s[:, ti * TAU:(ti + 1) * TAU], bk(5, 0, 128, 0, TAU)),
                            obufs=[CWB[ti]], ibufs=[BK[5]])
                for m in range(HCH):
                    ws = m % 2
                    dma("gp", wcat_s[:, ws * 2048:(ws + 1) * 2048], dr["wcat"][e][m],
                        obufs=[WCAT[ws]], sem=("wca", "wcb")[ws])
                    if dbg_state.get("w") is not None and m == 0:
                        dma("sy", dbg_state["w"], wcat_s[:, 0:2048], ibufs=[WCAT[0]],
                            sem="st")
                        dbg_state["w"] = None
                    for k in range(DCH):
                        w1k = wcat_s[:, ws * 2048 + k * 128: ws * 2048 + (k + 1) * 128]
                        w3k = wcat_s[:, ws * 2048 + 1024 + k * 128:
                                     ws * 2048 + 1024 + (k + 1) * 128]
                        for ti in range(2):
                            tau = 2 * tp + ti
                            xk = xln[:, k * TOK + tau * TAU: k * TOK + (tau + 1) * TAU]
                            mm(bk(ti), w1k, xk, BK[ti],
                               ibufs=[WCAT[ws], LNOUT[k]], start=(k == 0),
                               stop=(k == DCH - 1), inc=(k == DCH - 1))
                            mm(bk(2 + ti), w3k, xk, BK[2 + ti],
                               ibufs=[WCAT[ws], LNOUT[k]], start=(k == 0),
                               stop=(k == DCH - 1), inc=(k == DCH - 1))
                    for ti in range(2):
                        sl = (2 * m + ti) % 4
                        vop(lambda s, ti=ti, sl=sl: s.activation(
                            out=sgub[:, sl * TAU:(sl + 1) * TAU], in_=bk(ti),
                            func=AF.Silu), obufs=[SG[sl]], ibufs=[BK[ti]], eng="ac")
                        if dbg_state.get("sg") is not None and m == 0 and ti == 1:
                            dma("sy", dbg_state["sg"], sgub[:, 0:1024],
                                ibufs=[SG[0], SG[1]], sem="st")
                            dbg_state["sg"] = None
                        eob = EO[(m, ti)]
                        vop(lambda v, m=m, ti=ti, sl=sl: v.tensor_tensor(
                            out=eo[:, m * 1024 + ti * TAU: m * 1024 + (ti + 1) * TAU],
                            in0=sgub[:, sl * TAU:(sl + 1) * TAU],
                            in1=bk(2 + ti), op=ALU.mult), obufs=[eob],
                            ibufs=[SG[sl], BK[2 + ti]])
                        if e < E:
                            vop(lambda v, m=m, ti=ti: v.tensor_tensor(
                                out=eo[:, m * 1024 + ti * TAU: m * 1024 + (ti + 1) * TAU],
                                in0=eo[:, m * 1024 + ti * TAU: m * 1024 + (ti + 1) * TAU],
                                in1=cwb_s[:, ti * TAU:(ti + 1) * TAU], op=ALU.mult),
                                obufs=[eob], ibufs=[eob, CWB[ti]])
                        else:
                            vop(lambda v, m=m, ti=ti: v.tensor_scalar(
                                out=eo[:, m * 1024 + ti * TAU: m * 1024 + (ti + 1) * TAU],
                                in0=eo[:, m * 1024 + ti * TAU: m * 1024 + (ti + 1) * TAU],
                                scalar1=0.5, scalar2=None, op0=ALU.mult),
                                obufs=[eob], ibufs=[eob])
                if dbg_state.get("eo") is not None and tp == 0 and e == experts[0]:
                    dma("sy", dbg_state["eo"], eo[:, :],
                        ibufs=[EO[(m_, t_)] for m_ in range(HCH) for t_ in range(2)],
                        sem="st")
                    dbg_state["eo"] = None
                for dh in range(2):
                    for m in range(HCH):
                        ws = m % 4
                        dma("gp", w2b_s[:, ws * 512:(ws + 1) * 512],
                            dr["w2cat"][e][m][:, dh * 512:(dh + 1) * 512],
                            obufs=[W2B[ws]], sem=("w2a", "w2b", "w2c", "w2d")[ws])
                        for d4 in range(4):
                            for ti in range(2):
                                zi = d4 * 2 + ti
                                mm(bk(zi),
                                   w2b_s[:, ws * 512 + d4 * 128: ws * 512 + (d4 + 1) * 128],
                                   eo[:, m * 1024 + ti * TAU: m * 1024 + (ti + 1) * TAU],
                                   BK[zi], ibufs=[W2B[ws], EO[(m, ti)]],
                                   start=(m == 0), stop=(m == HCH - 1),
                                   inc=(m == HCH - 1) or (d4 == 3 and ti == 1))
                    for d4 in range(4):
                        for ti in range(2):
                            zi = d4 * 2 + ti
                            dchunk = dh * 4 + d4
                            tau = 2 * tp + ti
                            vop(lambda v, zi=zi, dc=dchunk, tau=tau: v.tensor_tensor(
                                out=resid[:, dc * TOK + tau * TAU:
                                          dc * TOK + (tau + 1) * TAU],
                                in0=resid[:, dc * TOK + tau * TAU:
                                          dc * TOK + (tau + 1) * TAU],
                                in1=bk(zi), op=ALU.add),
                                obufs=[RES[dchunk]], ibufs=[BK[zi], RES[dchunk]])

    def emit_p2():
        CCIN = buf("ccin")
        CCIN.begin_group()
        STGB = [buf("stg0"), buf("stg1")]
        SG[0].begin_group()
        SG[1].begin_group()
        for k in range(DCH):
            dma("gp", wdkvT_s[:, k * 128:(k + 1) * 128], dr["wdkvT"][k],
                obufs=[SG[0], SG[1]], group=True, sem="lmx1")
        for tau in range(NTAU):
            to = tau * TAU
            emit_ln(to, TAU, xln, TOK, to, 4, 5, 6, LNOUT)
            for tt in range(TAU // 128):
                gto = to + tt * 128
                sl = tt % 2
                for k in range(DCH):
                    mm(bk(7, 0, 128, 0, KVL),
                       xln[:, k * TOK + gto: k * TOK + gto + 128],
                       wdkvT_s[:, k * 128:(k + 1) * 128], BK[7],
                       ibufs=[LNOUT[k], SG[0], SG[1]], start=(k == 0),
                       stop=(k == DCH - 1),
                       inc=(k == DCH - 1))
                vop(lambda v, s=sl: v.tensor_copy(
                    stg_s[:, s * 128:(s + 1) * 128], bk(7, 0, 128, 0, KVL)),
                    obufs=[STGB[sl]], ibufs=[BK[7]])
                dma("sy", cc_in[gto:gto + 128, :], stg_s[:, sl * 128:(sl + 1) * 128],
                    obufs=[CCIN], ibufs=[STGB[sl]], group=True, sem="lms")
        cctok = P.op("gp", lambda g: g.collective_compute(
            "AllGather", ALU.bypass, replica_groups=[[0, 1, 2, 3], [4, 5, 6, 7]],
            ins=[cc_in.ap().opt()], outs=[cc_out.ap().opt()]),
            deps=CCIN.wdeps() + CCIN.rdeps(), inc="cc")
        CCOUT = buf("ccout")
        CCOUT.note_w(cctok)
        CKVM.begin_group()
        for k in range(T // 128):
            dma("sy", ckvM[:, k * 128:(k + 1) * 128],
                cc_out[gather_chunk_src(k):gather_chunk_src(k) + 128, :],
                obufs=[CKVM], ibufs=[CCOUT], group=True, sem="lmk")
        CKVT.begin_group()
        for k in range(T // 128):
            bi = 6 if (k % 2 == 0) else 7
            mm(bk(bi, 0, 128, 0, 128), ckvM[:, k * 128:(k + 1) * 128], identb_s[:, :],
               BK[bi], ibufs=[CKVM, cb["identb"]])
            vop(lambda v, k=k, bi=bi: v.tensor_copy(
                ckvT[:, k * 128:(k + 1) * 128], bk(bi, 0, 128, 0, 128)),
                obufs=[CKVT], ibufs=[BK[bi]], group=True)

    LNST = [LNOUT[0] for _ in range(DCH)]

    def emit_attn():
        if dbg_state.get("ckvm") is not None:
            dma("sy", dbg_state["ckvm"], ckvM[:, :], ibufs=[CKVM], sem="st")
            dma("sy", dbg_state["ckvt"], ckvT[:, :], ibufs=[CKVT], sem="st")
            dbg_state["ckvm"] = None
        for sgb in SG:
            sgb.begin_group()
        for k in range(DCH):
            dma("gp", wdqT_s[:, k * 128:(k + 1) * 128], dr["wdqT"][k],
                obufs=[SG[0], SG[1]], group=True, sem="lmx2")
            dma("gp", wuqT_s[:, k * 128:(k + 1) * 128],
                dr["wuqT"][:, k * 128:(k + 1) * 128], obufs=[SG[2], SG[3]], group=True,
                sem="lmx2")
        MEO = [EO[(mq, tq)] for mq in range(4) for tq in range(2)]
        for mb in MEO:
            mb.begin_group()
        dma("sy", masks_s[:, :], dr["masks"], obufs=MEO, group=True, sem="lmm")
        CKQ = LNOUT[0]
        C2 = buf("c2")
        CN = buf("cnorm")
        CN.begin_group()
        for piece in range(8):
            po = piece * 1024
            vop(lambda v, po=po: v.tensor_tensor(
                out=xln[:, 0:1024], in0=ckvT[:, po:po + 1024],
                in1=ckvT[:, po:po + 1024], op=ALU.mult), obufs=[CKQ], ibufs=[CKVT])
            for c in range(2):
                mm(bk(7, 0, 1, 0, 512), onescb_s[:, :],
                   xln[:, c * 512:(c + 1) * 512], BK[7], ibufs=[cb["onescb"], CKQ])
                vop(lambda v, piece=piece, c=c: v.tensor_reduce(
                    out=rows[0:1, piece * 2 + c:piece * 2 + c + 1],
                    in_=bk(7, 0, 1, 0, 512), axis=AX.X, op=ALU.max), obufs=[CN],
                    ibufs=[BK[7]], group=True)
        vop(lambda v: v.tensor_reduce(out=rows[0:1, 16:17], in_=rows[0:1, 0:16],
                                      axis=AX.X, op=ALU.max), obufs=[C2], ibufs=[CN])

        STGB2 = [buf("stg0"), buf("stg1")]
        QA, QT = buf("qa"), buf("cwqt")
        QL = [buf("ql0"), buf("ql1")]
        NB = [buf("nb0"), buf("nb1")]
        ZA = buf("zacc")
        CTXN = [buf("ctxn0"), buf("ctxn1")]
        PRB = [(EO[(4, 0)], EO[(4, 1)]), (EO[(5, 0)], EO[(5, 1)])]
        SCC = {n: buf("sc_" + n) for n in ("m2", "b2", "ln", "B", "nb")}
        ZR = buf("zrow")
        RZ = buf("rz")
        RZB = buf("rzb")

        for j in range(NST):
            so = j * STQ
            nch = 8 * (j + 1)
            ngr = nch // 4
            emit_ln(so, STQ, lnst_s, STQ, 0, 7, 6, 6, LNST)
            for k in range(DCH):
                mm(bk(7, 0, 128, 0, STQ), wdqT_s[:, k * 128:(k + 1) * 128],
                   lnst_s[:, k * STQ:(k + 1) * STQ], BK[7],
                   ibufs=[SG[0], SG[1], LNST[k]],
                   start=(k == 0), stop=(k == DCH - 1), inc=(k == DCH - 1))
            vop(lambda v: v.tensor_copy(qa_s[:, :], bk(7, 0, 128, 0, STQ)), obufs=[QA],
                ibufs=[BK[7]])
            QT.begin_group()
            for r in range(DCH):
                mm(bk(7, 0, 128, 0, STQ), wuqT_s[:, r * 128:(r + 1) * 128], qa_s[:, :],
                   BK[7], ibufs=[SG[2], SG[3], QA])
                vop(lambda v, r=r: v.tensor_copy(
                    qTst_s[:, r * STQ:(r + 1) * STQ], bk(7, 0, 128, 0, STQ)),
                    obufs=[QT], ibufs=[BK[7]], group=True)
            if dbg_state.get("qt") is not None and j == 0:
                dma("sy", dbg_state["keff"], keff_s[:, :], ibufs=[KEFF], sem="st")
                dma("sy", dbg_state["qa"], qa_s[:, :], ibufs=[QA], sem="st")
                dma("sy", dbg_state["qt"], qTst_s[:, :], ibufs=[QT], sem="st")
                dma("sy", dbg_state["lnst"], lnst_s[:, :],
                    ibufs=[LNST[0]], sem="st")
                dbg_state["qt"] = None
            def pair_prologue(pp):
                sl = pp % 2
                for hi in range(2):
                    h = 2 * pp + hi
                    lhs = keff_s[(h % 2) * 64:(h % 2) * 64 + 64,
                                 (h // 2) * 128:(h // 2) * 128 + 128]
                    rhs = qTst_s[(h % 2) * 64:(h % 2) * 64 + 64,
                                 (h // 2) * STQ:(h // 2) * STQ + STQ]
                    bki = 7 if hi == 0 else 6
                    mm(bk(bki, 0, 128, 0, STQ), lhs, rhs, BK[bki],
                       ibufs=[KEFF, QT], start=True, stop=True)
                vop(lambda v, sl=sl: v.tensor_copy(
                    qlT_s[:, sl * 512:sl * 512 + STQ], bk(7, 0, 128, 0, STQ)),
                    obufs=[QL[sl]], ibufs=[BK[7]])
                vop(lambda v, sl=sl: v.tensor_copy(
                    qlT_s[:, sl * 512 + STQ:sl * 512 + 512], bk(6, 0, 128, 0, STQ)),
                    obufs=[QL[sl]], ibufs=[BK[6]])
                if attn_level == 11:
                    return
                vop(lambda v, sl=sl: v.tensor_tensor(
                    out=stg_s[:, 0:256], in0=qlT_s[:, sl * 512:sl * 512 + 256],
                    in1=qlT_s[:, sl * 512:sl * 512 + 256], op=ALU.mult),
                    obufs=[STGB2[0], STGB2[1]], ibufs=[QL[sl]])
                mm(bk(7, 0, 1, 0, 256), onescb_s[:, :], stg_s[:, 0:256], BK[7],
                   ibufs=[cb["onescb"], STGB2[0], STGB2[1]])
                vop(lambda v, sl=sl: v.tensor_tensor(
                    out=stg_s[:, 0:256], in0=qlT_s[:, sl * 512 + 256:sl * 512 + 512],
                    in1=qlT_s[:, sl * 512 + 256:sl * 512 + 512], op=ALU.mult),
                    obufs=[STGB2[0], STGB2[1]], ibufs=[QL[sl], BK[7]])
                mm(bk(7, 0, 1, 256, 512), onescb_s[:, :], stg_s[:, 0:256], BK[7],
                   ibufs=[cb["onescb"], STGB2[0], STGB2[1]], start=True, stop=True)
                if attn_level == 12:
                    return
                vop(lambda v: v.tensor_reduce(
                    out=sc_s[0:1, 0:1], in_=bk(7, 0, 1, 0, 512), axis=AX.X, op=ALU.max),
                    obufs=[SCC["m2"]], ibufs=[BK[7]])
                vop(lambda v: v.tensor_tensor(out=sc_s[0:1, 1:2], in0=sc_s[0:1, 0:1],
                                              in1=rows[0:1, 16:17], op=ALU.mult),
                    obufs=[SCC["b2"]], ibufs=[SCC["m2"], C2])
                vop(lambda s: s.activation(out=sc_s[0:1, 2:3], in_=sc_s[0:1, 1:2],
                                           func=AF.Ln), obufs=[SCC["ln"]],
                    ibufs=[SCC["b2"]], eng="ac")
                vop(lambda s: s.activation(out=sc_s[0:1, 3:4], in_=sc_s[0:1, 2:3],
                                           func=AF.Exp, scale=0.5), obufs=[SCC["B"]],
                    ibufs=[SCC["ln"]], eng="ac")
                vop(lambda v: v.tensor_scalar(out=sc_s[0:1, 4:5], in0=sc_s[0:1, 3:4],
                                              scalar1=-1.0, scalar2=50.0, op0=ALU.mult,
                                              op1=ALU.add), obufs=[SCC["nb"]],
                    ibufs=[SCC["B"]])
                mm(bk(7, 0, 128, 300, 301), onesr_s[:, :], sc_s[0:1, 4:5], BK[7],
                   ibufs=[cb["onesr"], SCC["nb"]])
                vop(lambda v, sl=sl: v.tensor_copy(
                    negb_s[:, sl:sl + 1], bk(7, 0, 128, 300, 301)), obufs=[NB[sl]],
                    ibufs=[BK[7]])

            if attn_level >= 1:
                pair_prologue(0)
            pending_tail = [None]
            for pp in range(NH // 2):
                if attn_level < 1:
                    break
                sl = pp % 2
                cbank = 4 + sl
                ngr = nch // 2
                vop(lambda v: v.memset(zacc_s[:, 0:512], 0.0), obufs=[ZA], ibufs=[])

                def scores_group(g):
                    pair = g % 2
                    for q2 in range(2):
                        c = g * 2 + q2
                        hw = 2 * pair + q2
                        masked = (c >= nch - 8)
                        if masked:
                            mi = c - (nch - 8)
                            mm(bk(hw, 0, 128, 0, 512), identb_s[:, :],
                               masks_s[:, mi * 512:(mi + 1) * 512], BK[hw],
                               ibufs=[cb["identb"], MEO[mi]], start=True,
                               stop=False, inc=False)
                            mm(bk(hw, 0, 128, 0, 512),
                               ckvT[:, c * 128:(c + 1) * 128],
                               qlT_s[:, sl * 512:(sl + 1) * 512], BK[hw],
                               ibufs=[CKVT, QL[sl]], start=False, stop=True)
                        else:
                            mm(bk(hw, 0, 128, 0, 512),
                               ckvT[:, c * 128:(c + 1) * 128],
                               qlT_s[:, sl * 512:(sl + 1) * 512], BK[hw],
                               ibufs=[CKVT, QL[sl]], start=True, stop=True)

                def exp_g(g):
                    pair = g % 2
                    po = pair * 1024
                    vop(lambda s, po=po, pair=pair, sl=sl: s.activation(
                        out=probs_s[:, po:po + 1024], in_=PP[pair][:, :], func=AF.Exp,
                        bias=negb_s[:, sl:sl + 1], scale=1.0),
                        obufs=[PRB[pair][0], PRB[pair][1]],
                        ibufs=[BK[2 * pair], BK[2 * pair + 1], NB[sl]], eng="ac")

                def zacc_g(g):
                    pair = g % 2
                    po = pair * 1024
                    vop(lambda v, po=po: v.tensor_tensor(
                        out=zacc_s[:, 0:512], in0=zacc_s[:, 0:512],
                        in1=probs_s[:, po:po + 512], op=ALU.add), obufs=[ZA],
                        ibufs=[PRB[pair][0], ZA])
                    vop(lambda v, po=po: v.tensor_tensor(
                        out=zacc_s[:, 0:512], in0=zacc_s[:, 0:512],
                        in1=probs_s[:, po + 512:po + 1024], op=ALU.add), obufs=[ZA],
                        ibufs=[PRB[pair][1], ZA])

                def ctx_g(g):
                    pair = g % 2
                    po = pair * 1024
                    for q2 in range(2):
                        c = g * 2 + q2
                        mm(bk(cbank, 0, 128, 0, 512), ckvM[:, c * 128:(c + 1) * 128],
                           probs_s[:, po + q2 * 512:po + (q2 + 1) * 512], BK[cbank],
                           ibufs=[CKVM, PRB[pair][q2]], start=(c == 0),
                           stop=(c == nch - 1), inc=(c == nch - 1) or (q2 == 1))

                if attn_level >= 2:
                    scores_group(0)
                    for g in range(ngr):
                        if attn_level >= 3:
                            exp_g(g)
                        if attn_level >= 4:
                            zacc_g(g)
                        if g + 1 < ngr:
                            scores_group(g + 1)
                        if attn_level >= 5:
                            ctx_g(g)
                if pp + 1 < NH // 2 and attn_level >= 1:
                    pair_prologue(pp + 1)
                if attn_level < 5:
                    continue
                if pending_tail[0] is not None:
                    pending_tail[0]()
                    pending_tail[0] = None
                mm(bk(7, 0, 1, 0, 512), onescf_s[:, :], zacc_s[:, 0:512], BK[7],
                   ibufs=[cb["onescf"], ZA])
                vop(lambda v: v.tensor_copy(zr_s[0:1, 0:512], bk(7, 0, 1, 0, 512)),
                    obufs=[ZR], ibufs=[BK[7]])
                vop(lambda v: v.tensor_scalar(
                    out=zr_s[0:1, 0:512], in0=zr_s[0:1, 0:512], scalar1=1e-30,
                    scalar2=None, op0=ALU.add), obufs=[ZR], ibufs=[ZR])
                vop(lambda v: v.reciprocal(zr_s[0:1, 512:1024], zr_s[0:1, 0:512]),
                    obufs=[RZ], ibufs=[ZR])
                mm(bk(7, 0, 128, 0, 512), onesr_s[:, :], zr_s[0:1, 512:1024],
                   BK[7], ibufs=[cb["onesr"], RZ])
                vop(lambda v: v.tensor_copy(rzb_s[:, :], bk(7, 0, 128, 0, 512)),
                    obufs=[RZB], ibufs=[BK[7]])

                def tail(pp=pp, cbank=cbank, so=so):
                    vop(lambda v, cbk=cbank: v.tensor_tensor(
                        out=ctxn_s[:, 0:512], in0=bk(cbk, 0, 128, 0, 512),
                        in1=rzb_s[:, :], op=ALU.mult), obufs=[CTXN[0]],
                        ibufs=[BK[cbank], RZB])
                    h0, h1 = 2 * pp, 2 * pp + 1
                    mm(bk(6, 0, 64, 0, STQ), veff_s[:, h0 * 64:h0 * 64 + 64],
                       ctxn_s[:, 0:STQ], BK[6], ibufs=[VEFF, CTXN[0]],
                       start=True, stop=True, inc=False)
                    mm(bk(6, 64, 128, 0, STQ), veff_s[:, h1 * 64:h1 * 64 + 64],
                       ctxn_s[:, STQ:2 * STQ], BK[6], ibufs=[VEFF, CTXN[0]],
                       start=True, stop=True, tile_position=(0, 64))
                    dchunk = pp
                    vop(lambda v, dc=dchunk, so=so: v.tensor_tensor(
                        out=resid[:, dc * TOK + so: dc * TOK + so + STQ],
                        in0=resid[:, dc * TOK + so: dc * TOK + so + STQ],
                        in1=bk(6, 0, 128, 0, STQ), op=ALU.add),
                        obufs=[RES[dchunk]], ibufs=[BK[6], RES[dchunk]])

                pending_tail[0] = tail

            if pending_tail[0] is not None:
                pending_tail[0]()
                pending_tail[0] = None
    dbg_outs = []
    if debug:
        dbg_xln = nc.dram_tensor("dbg_xln", [128, DCH * TOK], BF16,
                                 kind="ExternalOutput").ap()
        dbg_cw8 = nc.dram_tensor("dbg_cw8", [E, TOK], BF16,
                                 kind="ExternalOutput").ap()
        dbg_eo = nc.dram_tensor("dbg_eo", [128, HCH * 1024], BF16,
                                 kind="ExternalOutput").ap()
        dbg_state["eo"] = dbg_eo
        dbg_w = nc.dram_tensor("dbg_w", [128, 2048], BF16, kind="ExternalOutput").ap()
        dbg_state["w"] = dbg_w
        dbg_sg = nc.dram_tensor("dbg_sg", [128, 1024], BF16, kind="ExternalOutput").ap()
        dbg_state["sg"] = dbg_sg
        dbg_ckvm = nc.dram_tensor("dbg_ckvm", [128, T], BF16, kind="ExternalOutput").ap()
        dbg_state["ckvm"] = dbg_ckvm
        dbg_ckvt = nc.dram_tensor("dbg_ckvt", [128, T], BF16, kind="ExternalOutput").ap()
        dbg_state["ckvt"] = dbg_ckvt
        dbg_nb = nc.dram_tensor("dbg_nb", [128, 2], F32, kind="ExternalOutput").ap()
        dbg_state["nb"] = dbg_nb
        dbg_rows = nc.dram_tensor("dbg_rows", [1, 64], F32, kind="ExternalOutput").ap()
        dbg_state["rows"] = dbg_rows
        dbg_sc = nc.dram_tensor("dbg_sc", [1, 16], F32, kind="ExternalOutput").ap()
        dbg_state["sc"] = dbg_sc
        dbg_ql = nc.dram_tensor("dbg_ql", [128, 2 * STQ], BF16, kind="ExternalOutput").ap()
        dbg_state["ql"] = dbg_ql
        dbg_keff = nc.dram_tensor("dbg_keff", [128, DCH * 128], BF16, kind="ExternalOutput").ap()
        dbg_state["keff"] = dbg_keff
        dbg_qa = nc.dram_tensor("dbg_qa", [128, STQ], BF16, kind="ExternalOutput").ap()
        dbg_state["qa"] = dbg_qa
        dbg_qt = nc.dram_tensor("dbg_qt", [128, DCH * STQ], BF16, kind="ExternalOutput").ap()
        dbg_state["qt"] = dbg_qt
        dbg_lnst = nc.dram_tensor("dbg_lnst", [128, DCH * STQ], BF16, kind="ExternalOutput").ap()
        dbg_state["lnst"] = dbg_lnst

    if "ffn1" in phases:
        emit_ffn()
        if debug:
            dma("sy", dbg_xln, xln[:, :], ibufs=[LNOUT[k] for k in range(DCH)],
                sem="st")
            dma("sy", dbg_cw8, cw8[:, :], ibufs=[CW8], sem="st")
    if "p2" in phases:
        emit_p2()
    if "attn" in phases:
        emit_attn()
    if "ffn2" in phases:
        emit_ffn()

    OUTB = buf("out")
    OUTB.begin_group()
    for k in range(DCH):
        dma("sy", out_d[k * 128:(k + 1) * 128, :], resid[:, k * TOK:(k + 1) * TOK],
            obufs=[OUTB], ibufs=[RES[k]], sem="st", group=True)

    semnames = ("lm0", "lmo", "lms", "lmk", "lmm", "lmx0", "lmx1", "lmx2",
            "wca", "wcb", "w2a", "w2b", "w2c", "w2d", "pe", "ve", "ac", "cc", "st")
    with ExitStack() as ses:
        SEM = {n: ses.enter_context(nc.semaphore("s_" + n)) for n in semnames}
        block = ses.enter_context(nc.Block())

        def run(engine_obj, qname):
            for build, waits, inc, amt in P.q[qname]:
                for s, v in waits:
                    engine_obj.wait_ge(SEM[s], v)
                inst = build(engine_obj)
                if inc is not None:
                    inst.then_inc(SEM[inc], amt)

        @block.tensor
        def _(t):
            run(t, "pe")

        @block.vector
        def _(v):
            run(v, "ve")

        @block.scalar
        def _(s):
            run(s, "ac")

        @block.gpsimd
        def _(g):
            run(g, "gp")

        @block.sync
        def _(sy):
            run(sy, "sy")
            sy.wait_ge(SEM["st"], P.ctr["st"])

    es.close()
    return nc


_prog_cache = {}
PHASES = ("ffn1", "p2", "attn", "ffn2")
DEBUG = False
LAST_RES = {}


EXPERTS = tuple(range(NEXP))


ATTN_LEVEL = 5


def _get_prog():
    key = (PHASES, DEBUG, EXPERTS, ATTN_LEVEL)
    if key not in _prog_cache:
        _prog_cache[key] = build_program(PHASES, DEBUG, EXPERTS, ATTN_LEVEL)
    return _prog_cache[key]


def kernel(L0_attn_x, L0_attn_W_dq, L0_attn_W_uq, L0_attn_W_dkv, L0_attn_W_uk,
           L0_attn_W_uv, L0_attn_W_o, L0_ffn_router_w, L0_ffn_experts_bias,
           L0_ffn_w1, L0_ffn_w3, L0_ffn_w2, L0_ffn_sw1, L0_ffn_sw3, L0_ffn_sw2):
    f = lambda a: np.ascontiguousarray(np.asarray(a, dtype=np.float32))
    x = f(L0_attn_x)
    w1a = np.concatenate([f(L0_ffn_w1), f(L0_ffn_sw1)], axis=0)
    w3a = np.concatenate([f(L0_ffn_w3), f(L0_ffn_sw3)], axis=0)
    w2a = np.concatenate([f(L0_ffn_w2), f(L0_ffn_sw2)], axis=0)

    def mk(w):
        r = w.reshape(NEXP, DCH, 128, HCH, 128)
        return r.transpose(0, 3, 2, 1, 4).reshape(NEXP, HCH, 128, DCH * 128)

    wcat = np.ascontiguousarray(np.concatenate([mk(w1a), mk(w3a)], axis=3))
    w2cat = np.ascontiguousarray(w2a.reshape(NEXP, HCH, 128, D))

    wdq = f(L0_attn_W_dq)
    wdqT = np.ascontiguousarray(wdq.T.reshape(DCH, 128, 128))
    wuqT = np.ascontiguousarray(f(L0_attn_W_uq).T)
    wdkvT = np.ascontiguousarray(f(L0_attn_W_dkv).T.reshape(DCH, 128, 128))
    wukr = np.ascontiguousarray(f(L0_attn_W_uk).reshape(DCH, 128, KVL))
    wuvr = np.ascontiguousarray(f(L0_attn_W_uv).reshape(DCH, 128, KVL))
    woT = np.ascontiguousarray(f(L0_attn_W_o).T.reshape(DCH, 128, D))
    routT = np.ascontiguousarray(f(L0_ffn_router_w).T.reshape(DCH, 128, E))
    bias8 = np.ascontiguousarray(f(L0_ffn_experts_bias).reshape(E, 1))

    identb = np.eye(128, dtype=np.float32).astype(bfnp)
    identf = np.eye(128, dtype=np.float32)
    onesr = np.ones((1, 128), np.float32)
    onescb = np.ones((128, 1), np.float32).astype(bfnp)
    onescf = np.ones((128, 1), np.float32)
    onehot = np.zeros((E, E * 128), np.float32)
    for e in range(E):
        onehot[e, e * 128:(e + 1) * 128] = 1.0
    onehot = onehot.astype(bfnp)

    pos = _local_positions()
    in_maps = []
    for core in range(NCORE):
        b = core // 4
        ct = core % 4
        xT = np.ascontiguousarray(x[b][pos[ct], :].T)
        in_maps.append(dict(
            xT=xT, wcat=wcat, w2cat=w2cat, wdqT=wdqT, wuqT=wuqT, wdq=wdq,
            wdkvT=wdkvT, wuk=wukr, wuv=wuvr, woT=woT, routT=routT, bias8=bias8,
            masks=make_masks(ct), identb=identb, identf=identf, onesr=onesr,
            onescb=onescb, onescf=onescf, onehot=onehot,
        ))

    nc = _get_prog()
    res = run_bass_kernel_spmd(nc, in_maps, core_ids=list(range(NCORE)))
    LAST_RES["res"] = res

    out = np.empty((B, T, C), dtype=np.float32)
    for core in range(NCORE):
        b = core // 4
        ct = core % 4
        out[b][pos[ct], :] = res.results[core]["outT"].T
    return out
